# version 06
# baseline (speedup 1.0000x reference)
import math
import numpy as np

import concourse.bass as bass
import concourse.mybir as mybir
from concourse.bass_utils import run_bass_kernel_spmd
from concourse.alu_op_type import AluOpType as OP

AF = mybir.ActivationFunctionType
F32 = mybir.dt.float32
BF16 = mybir.dt.bfloat16

NCORES = 8
B, T, HID = 16, 2048, 1024
H, D, V = 8, 64, 128
DH = 4096
TOK = (B // NCORES) * T
STOK = 512
NSUB = TOK // STOK
SUB_PER_SEQ = T // STOK
EPS = 1e-5

ENGINES = ("tensor", "vector", "scalar", "sync", "gpsimd")

RSQRT_TANGENTS = [((1.0 / (2.0 * math.sqrt(a))),
                   (math.sqrt(a) / 2.0) + EPS / (2.0 * math.sqrt(a)))
                  for a in (10.0 ** e for e in range(-5, 5))]

DEBUG_TAPS = False
BUILD_NSUB = NSUB
BUILD_PHASE = "full"
BUILD_ATTN_STAGES = 99


class Sched:

    def __init__(self):
        self.ops = {e: [] for e in ENGINES}
        self.count = {}
        self.last_wait = {e: {} for e in ENGINES}

    def sem(self, name):
        self.count.setdefault(name, 0)

    def emit(self, eng, fn, waits=(), inc=None, inc_n=1):
        ws = []
        for w in waits:
            if w is None:
                continue
            s, v = w
            if self.last_wait[eng].get(s, -1) < v:
                ws.append((s, v))
                self.last_wait[eng][s] = v
        ev = None
        if inc is not None:
            self.count[inc] += inc_n
            ev = (inc, self.count[inc])
        self.ops[eng].append((ws, fn, inc, inc_n))
        return ev


def emax(*evs):
    return [e for e in evs if e is not None]


def host_prepare(inputs):
    f = np.float32
    d = {}
    Wq = inputs["L0_attn_Wq"]; Wk = inputs["L0_attn_Wk"]
    Wv = inputs["L0_attn_Wv"]; Wg = inputs["L0_attn_Wg"]
    Wo = inputs["L0_attn_Wo"]
    d["wq_l"] = np.ascontiguousarray(Wq.reshape(512, 8, 128).transpose(2, 1, 0))
    d["wk_l"] = np.ascontiguousarray(Wk.reshape(512, 8, 128).transpose(2, 1, 0))
    d["wv_l"] = np.ascontiguousarray(Wv.reshape(1024, 8, 128).transpose(2, 1, 0))
    d["wg_l"] = np.ascontiguousarray(Wg.reshape(1024, 8, 128).transpose(2, 1, 0))
    d["wo_l"] = np.ascontiguousarray(Wo.reshape(1024, 8, 128).transpose(1, 2, 0))
    W1 = inputs["L0_ffn_w1_w"]; W2 = inputs["L0_ffn_w2_w"]
    H1 = inputs["L0_ffn_h1_w"]; H2 = inputs["L0_ffn_h2_w"]
    W3 = inputs["L0_ffn_w3_w"]
    d["w1_l"] = np.ascontiguousarray(W1.reshape(32, 128, 8, 128).transpose(3, 0, 2, 1))
    d["w2_l"] = np.ascontiguousarray(W2.reshape(32, 128, 8, 128).transpose(3, 0, 2, 1))
    d["h1_l"] = np.ascontiguousarray(H1.reshape(32, 128, 32, 128).transpose(3, 0, 2, 1))
    d["h2_l"] = np.ascontiguousarray(H2.reshape(32, 128, 32, 128).transpose(3, 0, 2, 1))
    d["w3_l"] = np.ascontiguousarray(W3.reshape(1024, 32, 128).transpose(2, 1, 0))
    d["s1_l"] = np.ascontiguousarray(inputs["L0_ffn_w1_s"].reshape(32, 128).T)
    d["s2_l"] = np.ascontiguousarray(inputs["L0_ffn_w2_s"].reshape(32, 128).T)
    d["s1h_l"] = np.ascontiguousarray(inputs["L0_ffn_h1_s"].reshape(32, 128).T)
    d["s2h_l"] = np.ascontiguousarray(inputs["L0_ffn_h2_s"].reshape(32, 128).T)
    d["s3B"] = np.ascontiguousarray(np.tile(inputs["L0_ffn_w3_s"][None, :], (128, 1)))
    d["gnwB"] = np.ascontiguousarray(np.tile(inputs["L0_attn_gn_weight"][None, :], (128, 1)))
    d["gnbB"] = np.ascontiguousarray(np.tile(inputs["L0_attn_gn_bias"][None, :], (128, 1)))

    inv32 = (1.0 / 10000.0 ** (np.arange(0, D, 2, dtype=np.float64) / D)).astype(f)
    tpos = np.arange(T, dtype=f)
    fr = (tpos[:, None] * inv32[None, :]).astype(f)
    cos32 = np.cos(fr).astype(np.float64)
    sin32 = np.sin(fr).astype(np.float64)
    gamma = 1.0 - 2.0 ** (-5.0 - np.arange(H, dtype=np.float64))
    j = (np.arange(T) % 128).astype(np.float64)
    gq = gamma[None, :] ** (j[:, None] + 1) * D ** -0.5
    gk = gamma[None, :] ** (-(j[:, None] + 1))

    def rope_tables(gscale):
        cosT = np.empty((T, H, 2, 32), np.float64)
        sinT = np.empty((T, H, 2, 32), np.float64)
        for h in range(H):
            cosT[:, h, 0, :] = cos32 * gscale[:, h:h + 1]
            cosT[:, h, 1, :] = cos32 * gscale[:, h:h + 1]
            sinT[:, h, 0, :] = -sin32 * gscale[:, h:h + 1]
            sinT[:, h, 1, :] = sin32 * gscale[:, h:h + 1]
        return (np.ascontiguousarray(cosT.reshape(T, 512).astype(f)),
                np.ascontiguousarray(sinT.reshape(T, 512).astype(f)))

    d["ropeQc"], d["ropeQs"] = rope_tables(gq)
    d["ropeKc"], d["ropeKs"] = rope_tables(gk)
    m = (np.arange(128)[None, :] >= np.arange(128)[:, None]).astype(f)
    d["mask4"] = np.ascontiguousarray(np.tile(m, (1, 4)))
    gC = (gamma ** 128).astype(f)
    d["gCB"] = np.ascontiguousarray(np.tile(np.repeat(gC, 128)[None, :], (128, 1)))
    d["ident"] = np.eye(128, dtype=f)
    cp = np.zeros((128, 8), f)
    cp[:, 0] = EPS
    cp[:, 1] = 1.0 / 1024.0
    d["cpack"] = cp
    return d




class Builder:
    def __init__(self):
        self.nc = bass.Bass("TRN2", target_bir_lowering=False)
        self.s = Sched()
        for name in ("pe", "dv", "ac", "cst", "sua", "sub", "suc", "sud", "suw", "hf0", "hf1",
                     "xl0", "xl1", "st0", "st1", "rp",
                     "sm0", "sm1", "sm2", "sm3", "hs0", "hs1"):
            self.s.sem(name)
        self._declare_io()
        self._alloc()

    def _declare_io(self):
        nc = self.nc

        def dt(n, sh, kind="ExternalInput"):
            return nc.dram_tensor(n, sh, F32, kind=kind).ap()

        self.x_in = dt("x", [TOK, HID])
        self.out = dt("out", [TOK, HID], kind="ExternalOutput")
        self.wq_d = dt("wq_l", [128, 8, 512])
        self.wk_d = dt("wk_l", [128, 8, 512])
        self.wv_d = dt("wv_l", [128, 8, 1024])
        self.wg_d = dt("wg_l", [128, 8, 1024])
        self.wo_d = dt("wo_l", [8, 128, 1024])
        self.w1_d = dt("w1_l", [128, 32, 8, 128])
        self.w2_d = dt("w2_l", [128, 32, 8, 128])
        self.h1_d = dt("h1_l", [128, 32, 32, 128])
        self.h2_d = dt("h2_l", [128, 32, 32, 128])
        self.w3_d = dt("w3_l", [128, 32, 1024])
        self.s1_d = dt("s1_l", [128, 32])
        self.s2_d = dt("s2_l", [128, 32])
        self.s1h_d = dt("s1h_l", [128, 32])
        self.s2h_d = dt("s2h_l", [128, 32])
        self.s3B_d = dt("s3B", [128, 1024])
        self.gnwB_d = dt("gnwB", [128, 1024])
        self.gnbB_d = dt("gnbB", [128, 1024])
        self.ropeQc_d = dt("ropeQc", [T, 512])
        self.ropeQs_d = dt("ropeQs", [T, 512])
        self.ropeKc_d = dt("ropeKc", [T, 512])
        self.ropeKs_d = dt("ropeKs", [T, 512])
        self.mask4_d = dt("mask4", [128, 512])
        self.gCB_d = dt("gCB", [128, 1024])
        self.ident_d = dt("ident", [128, 128])
        self.cpack_d = dt("cpack", [128, 8])

        def dsc(n, sh):
            return nc.dram_tensor(n, sh, BF16, kind="Internal").ap()

        self.wo_s = dsc("wo_scr", [8, 128, 1024])
        self.w1_s = dsc("w1_scr", [128, 32, 8, 128])
        self.w2_s = dsc("w2_scr", [128, 32, 8, 128])
        self.h1_s = dsc("h1_scr", [128, 32, 32, 128])
        self.h2_s = dsc("h2_scr", [128, 32, 32, 128])
        self.w3_s = dsc("w3_scr", [128, 32, 1024])
        self.tap_d = {}
        if DEBUG_TAPS:
            for name, sh, ty in [("t_xln", [128, 1024], BF16), ("t_qtm", [128, 512], BF16),
                             ("t_ktm", [128, 512], BF16), ("t_vtm", [128, 1024], BF16),
                             ("t_scM", [128, 1024], BF16), ("t_og", [128, 1024], BF16),
                             ("t_x1", [128, 1024], F32), ("t_x2fm", [128, 2048], BF16),
                             ("t_y1", [128, 512], BF16), ("t_hb", [128, 512], BF16),
                             ("t_ogfm", [128, 1024], BF16), ("t_wo0", [128, 1024], BF16)]:
                self.tap_d[name] = nc.dram_tensor(name, sh, ty, kind="ExternalOutput").ap()

    def _alloc(self):
        nc = self.nc

        def sb(n, sh, ty):
            return nc.alloc_sbuf_tensor(n, sh, ty).ap()

        self.wq = sb("wq", [128, 8, 512], BF16)
        self.wk = sb("wk", [128, 8, 512], BF16)
        self.wv = sb("wv", [128, 8, 1024], BF16)
        self.wg = sb("wg", [128, 8, 1024], BF16)
        self.xsb = [sb("x_a", [128, 4, 1024], F32), sb("x_b", [128, 4, 1024], F32)]
        self.gnwB = sb("gnwB_s", [128, 1024], F32)
        self.gnbB = sb("gnbB_s", [128, 1024], F32)
        self.s3B = sb("s3B_s", [128, 1024], F32)
        self.gCB = sb("gCB_s", [128, 1024], F32)
        self.mask4 = sb("mask4_s", [128, 512], BF16)
        self.s1 = sb("s1_s", [128, 32], F32)
        self.s2 = sb("s2_s", [128, 32], F32)
        self.s1h = sb("s1h_s", [128, 32], F32)
        self.s2h = sb("s2h_s", [128, 32], F32)
        self.cpack = sb("cpack_s", [128, 8], F32)
        self.identb = sb("identb", [128, 128], BF16)
        self.rope = sb("rope", [128, 4, 512], F32)
        self.S32 = sb("S32", [64, 1024], F32)
        self.spad = sb("spad", [128, 1024], BF16)
        self.arena = sb("arena", [128, 16384], BF16)
        a = self.arena
        self.xln_tm = a[:, 0:1024]
        self.xlnFM = a[:, 1024:2048]
        self.qtm = a[:, 2048:2560]
        self.ktm = a[:, 2560:3072]
        self.QFM = a[:, 3072:3584]
        self.KFM = a[:, 3584:4096]
        self.v_tm = a[:, 4096:5120]
        self.Gs = a[:, 5120:6144]
        self.Gsw = a[:, 6144:7168]
        self.Gsb = a[:, 7168:8192]
        self.scM = a[:, 8192:9216]
        self.on_all = a[:, 9216:10240]
        self.og = a[:, 10240:11264]
        self.ogFM = a[:, 11264:12288]
        self.qa32 = a[:, 12288:13312].bitcast(F32)
        self.qb32 = a[:, 13312:14336].bitcast(F32)
        self.o_sb = a[:, 14336:16384].bitcast(F32)
        self.y1 = a.rearrange("p (o t) -> p o t", o=32, t=512)
        self.hb = sb("hb", [128, 32, 512], BF16)
        self.x2FM = self.hb[:, 0:8, :]
        self.wbuf = sb("wbuf", [128, 12288], BF16)
        self.hslab = [self.wbuf[:, 0:4096].rearrange("p (i o) -> p i o", i=32, o=128),
                      self.wbuf[:, 4096:8192].rearrange("p (i o) -> p i o", i=32, o=128)]
        self.small = [self.wbuf[:, 8192 + i * 1024: 8192 + (i + 1) * 1024] for i in range(4)]
        self.small_iv = [s.rearrange("p (i o) -> p i o", i=8, o=128) for s in self.small]
        self.abf = sb("abf", [128, 2, 512], BF16)
        self.w3scr = self.abf.rearrange("p a b -> p (a b)")[:, 0:1024].bitcast(F32)
        self.dumps = [sb("dump0", [128, 1024], BF16), sb("dump1", [128, 1024], BF16)]
        self._dump_i = 0
        self._dump_ev = [None, None]
        self.lst = sb("lst", [128, 16], F32)
        self.lst2 = sb("lst2", [128, 16], F32)
        self.lnm = sb("lnm", [128, 16], F32)
        self.gst = sb("gst", [128, 16], F32)
        self.gmr = sb("gmr", [128, 16], F32)
        self.rs_s = sb("rs_s", [128, 64], F32)
        self.rs_b = sb("rs_b", [128, 64], F32)
        self.rs_v = sb("rs_v", [128, 64], F32)
        self.rs_t = sb("rs_t", [128, 64], F32)
        self.junk = sb("junk", [128, 64], F32)

        def ps(n, sh, ty):
            return nc.alloc_psum_tensor(n, sh, ty).ap()

        self.P0 = ps("P0", [128, 512], F32)
        self.P1 = ps("P1", [128, 512], F32)
        self.P23 = ps("P23", [128, 1024], F32)
        self.P45 = ps("P45", [128, 1024], F32)
        self.PT = ps("PT", [128, 1024], BF16)
        self.PS = self.P45[0:64, :]

    _tap_pending = None

    def _tw(self, waits):
        if self._tap_pending is None:
            return waits
        return list(waits) + [self._tap_pending]

    def t(self, fn, waits=(), inc=None):
        return self.s.emit("tensor", fn, self._tw(waits), inc)

    def v(self, fn, waits=(), inc=None):
        return self.s.emit("vector", fn, self._tw(waits), inc)

    def a(self, fn, waits=(), inc=None):
        return self.s.emit("scalar", fn, self._tw(waits), inc)

    def g(self, fn, waits=(), inc=None, inc_n=1):
        return self.s.emit("gpsimd", fn, waits, inc, inc_n)

    def gd(self, fn, waits=(), inc=None, inc_n=16):
        return self.s.emit("gpsimd", fn, waits, inc, inc_n)

    def q(self, fn, waits=(), inc=None, inc_n=16):
        return self.s.emit("sync", fn, waits, inc, inc_n)

    def qa_(self, fn, waits=(), inc=None, inc_n=16):
        return self.s.emit("scalar", fn, waits, inc, inc_n)

    def tap(self, name, src_ap, waits):
        if DEBUG_TAPS and name in self.tap_d:
            out_ap = self.tap_d.pop(name)
            ev = self.gd(lambda e, o=out_ap, s=src_ap: e.dma_start(out=o, in_=s),
                         waits=waits, inc="st0")
            self._tap_pending = ev

    def spacer(self):
        self.v(lambda e: e.tensor_scalar(self.junk[:], self.junk[:], 0.0, None, OP.mult))

    def rsqrt(self, out_ap, v_ap, n, so=0):
        sA = self.rs_s[:, so:so + n]
        sB = self.rs_s[:, so + n:so + 2 * n]
        tA = self.rs_t[:, so:so + n]
        tB = self.rs_t[:, so + n:so + 2 * n]
        bA = self.rs_b[:, so:so + n]
        bB = self.rs_b[:, so + n:so + 2 * n]
        vh = self.rs_v[:, so:so + n]
        self.spacer()
        T = RSQRT_TANGENTS
        self.v(lambda e: e.tensor_scalar(sA, v_ap, T[0][0], T[0][1], OP.mult, OP.add))
        self.v(lambda e: e.tensor_scalar(sB, v_ap, T[1][0], T[1][1], OP.mult, OP.add))
        for i in range(2, len(T), 2):
            c1a, c2a = T[i]
            c1b, c2b = T[i + 1]
            self.v(lambda e, c1a=c1a, c2a=c2a: e.tensor_scalar(tA, v_ap, c1a, c2a, OP.mult, OP.add))
            self.v(lambda e, c1b=c1b, c2b=c2b: e.tensor_scalar(tB, v_ap, c1b, c2b, OP.mult, OP.add))
            self.v(lambda e: e.tensor_tensor(sA, sA, tA, OP.min))
            self.v(lambda e: e.tensor_tensor(sB, sB, tB, OP.min))
        self.v(lambda e: e.tensor_scalar(vh, v_ap, 0.5, 0.5 * EPS, OP.mult, OP.add))
        self.v(lambda e: e.tensor_tensor(sA, sA, sB, OP.min))
        self.spacer()
        self.v(lambda e: e.reciprocal(sA, sA))
        nit = 4
        for i in range(nit):
            self.spacer()
            self.v(lambda e: e.tensor_mul(bA, sA, sA))
            self.spacer()
            self.v(lambda e: e.tensor_mul(bB, bA, vh))
            self.spacer()
            if i == nit - 1:
                return self.v(lambda e: e.scalar_tensor_tensor(
                    out_ap, bB, 1.5, sA, OP.subtract, OP.mult), inc="dv")
            self.v(lambda e: e.scalar_tensor_tensor(sA, bB, 1.5, sA, OP.subtract, OP.mult))

    def stat_op(self, in_ap, func, acc_ap, extra_waits=()):
        i = self._dump_i % 2
        self._dump_i += 1
        dmp = self.dumps[i][:, 0:in_ap.shape[-1]]
        ev = self.a(lambda e: e.activation(dmp, in_ap, func, accum_out=acc_ap),
                    waits=emax(self._dump_ev[i], *extra_waits), inc="ac")
        self._dump_ev[i] = ev
        return ev

    def build(self):
        self._setup()
        for s in range(BUILD_NSUB):
            self._subtile(s)
        dma_sems = ("st0", "st1", "xl0", "xl1", "rp", "sm0", "sm1", "sm2",
                    "sm3", "hs0", "hs1", "hf0", "hf1", "sua", "sub", "suc", "sud",
                    "suw", "cst")
        self.g(lambda e: e.nop(),
               waits=[(s, self.s.count[s]) for s in dma_sems])
        self._replay()
        return self.nc

    def _setup(self):
        ev = None
        for dst, src in [
            (self.gnwB, self.gnwB_d), (self.gnbB, self.gnbB_d),
            (self.s3B, self.s3B_d), (self.gCB, self.gCB_d),
            (self.s1, self.s1_d),
            (self.s2, self.s2_d), (self.s1h, self.s1h_d),
            (self.s2h, self.s2h_d), (self.cpack, self.cpack_d),
        ]:
            ev = self.q(lambda e, d=dst, s=src: e.dma_start(out=d, in_=s), inc="cst")
        self.ev_consts = ev
        self.gd(lambda e: e.dma_start(out=self.identb[:], in_=self.ident_d), inc="sua")
        self.gd(lambda e: e.dma_start(out=self.mask4[:], in_=self.mask4_d), inc="sua")
        self.gd(lambda e: e.dma_start(out=self.wq[:], in_=self.wq_d), inc="sua")
        self.gd(lambda e: e.dma_start(out=self.wk[:], in_=self.wk_d), inc="sua")
        self.gd(lambda e: e.dma_start(out=self.wv[:], in_=self.wv_d), inc="sua")
        evA = self.gd(lambda e: e.dma_start(out=self.wg[:], in_=self.wg_d), inc="sua")
        self.ev_mask = self.ev_attnw = evA
        self.gd(lambda e: e.dma_start(out=self.wo_s, in_=self.wo_d), inc="sub")
        self.gd(lambda e: e.dma_start(out=self.w1_s, in_=self.w1_d), inc="sub")
        evB = self.gd(lambda e: e.dma_start(out=self.w2_s, in_=self.w2_d), inc="sub")
        self.ev_wo_s = self.ev_w1_s = self.ev_w2_s = evB
        evC = self.gd(lambda e: e.dma_start(out=self.w3_s, in_=self.w3_d), inc="suw")
        self.ev_w3_s = evC
        self.ev_h1_s = [None] * 4
        self.ev_h2_s = [None] * 4
        self.v(lambda e: e.memset(self.junk[:], 0.0))
        self.v(lambda e: e.memset(self.S32[:], 0.0))
        self.ev_spad = self.v(lambda e: e.memset(self.spad[:], 0.0), inc="dv")
        self.xv = self.x_in.rearrange("(s tg p) i -> s p tg i", p=128, tg=4)
        self.outv = self.out.rearrange("(s tg p) i -> s p tg i", p=128, tg=4)
        self.ev_xload = [None] * (NSUB + 2)
        for s in range(2):
            self.ev_xload[s] = self.qa_(
                lambda e, s=s: e.dma_start(out=self.xsb[s % 2][:], in_=self.xv[s]),
                inc="xl%d" % (s % 2))
        self.ev_rope_read = None
        self.ev_rope = self._load_rope(0, deps=[])
        self.bank_free = {k: None for k in ("P0", "P1", "P23", "P45", "PT", "PS")}
        self.ev_arena_free = None
        self.ev_hb_free = None
        self.ev_small_free = [None] * 4
        self.ev_hslab_free = [None, None]
        self.ev_abf_free = [None, None]
        self.ev_x2fm_free = None
        self.ev_lst2_free = None
        self.ev_lst_free = None
        self.pending_ln1 = {}

    def _load_rope(self, tloc, deps):
        evs = None
        for i, tab in enumerate((self.ropeQc_d, self.ropeQs_d, self.ropeKc_d, self.ropeKs_d)):
            evs = self.qa_(
                lambda e, i=i, tab=tab, tloc=tloc: e.dma_start(
                    out=self.rope[:, i, :], in_=tab[tloc:tloc + 128, :]),
                waits=deps if i == 0 else (), inc="rp")
        return evs


    def _attn_front(self, s, tg):
        xs = self.xsb[s % 2]
        x_tg = xs[:, tg, :]
        st = {"x_tg": x_tg}
        m = self.lnm[:, tg:tg + 1]
        r = self.lnm[:, 4 + tg:5 + tg]
        prev = self._atn
        ev_apply = self.v(
            lambda e, x_tg=x_tg, m=m, r=r: e.tensor_scalar(
                self.xln_tm, x_tg, m, r, OP.subtract, OP.mult),
            waits=emax(self.ev_ln1_chain,
                       self.ev_arena_free if tg == 0 else None), inc="dv")
        self.tap("t_xln", self.xln_tm, emax(ev_apply))
        ev_tp = None
        for c in range(8):
            ev_tp = self.t(
                lambda e, c=c: e.transpose(self.PT[:, c * 128:(c + 1) * 128],
                                           self.xln_tm[:, c * 128:(c + 1) * 128],
                                           self.identb[:]),
                waits=emax(ev_apply, self.bank_free["PT"], self.ev_attnw) if c == 0 else (),
                inc="pe" if c == 7 else None)
        st["ev_tp"] = ev_tp
        ev_xfm = None
        for c in range(8):
            ev_xfm = self.v(
                lambda e, c=c: e.tensor_copy(self.xlnFM[:, c * 128:(c + 1) * 128],
                                             self.PT[:, c * 128:(c + 1) * 128]),
                waits=emax(ev_tp, prev["ev_vg"] if prev else None) if c == 0 else (),
                inc="dv" if c == 7 else None)
        self.bank_free["PT"] = ev_xfm
        st["ev_xfm"] = ev_xfm
        st["evQ"] = self._proj(self.P0[:], self.wq, 0, 512, ev_xfm, self.bank_free["P0"])
        st["evK"] = self._proj(self.P1[:], self.wk, 0, 512, ev_xfm, self.bank_free["P1"])
        return st

    def _proj(self, psum, wsb, n0, n1, ev_xfm, extra):
        ev = None
        for ic in range(8):
            ev = self.t(
                lambda e, ic=ic, psum=psum, wsb=wsb, n0=n0, n1=n1: e.matmul(
                    psum, self.xlnFM[:, ic * 128:(ic + 1) * 128],
                    wsb[:, ic, n0:n1], start=(ic == 0), stop=(ic == 7)),
                waits=emax(ev_xfm, extra) if ic == 0 else (),
                inc="pe" if ic == 7 else None)
        return ev

    def _attn_mid_a(self, s, tg, st):
        st["evVa"] = self._proj(self.P23[:, 0:512], self.wv, 0, 512, st["ev_xfm"], self.bank_free["P23"])
        st["evVb"] = self._proj(self.P23[:, 512:1024], self.wv, 512, 1024, st["ev_xfm"], None)
        evGa = self._proj(self.P45[:, 0:512], self.wg, 0, 512, st["ev_xfm"], self.bank_free["P45"])
        evGb = self._proj(self.P45[:, 512:1024], self.wg, 512, 1024, st["ev_xfm"], None)
        st["ev_vg"] = evGb
        self.a(lambda e: e.activation(self.Gs[:, 0:512], self.P45[:, 0:512], AF.Silu),
               waits=emax(evGa))
        st["evsl"] = self.a(
            lambda e: e.activation(self.Gs[:, 512:1024], self.P45[:, 512:1024], AF.Silu),
            waits=emax(evGb), inc="ac")

    def _attn_mid_b(self, s, tg, st):
        evgw = self.v(lambda e: e.tensor_mul(self.Gsw, self.Gs, self.gnwB[:]),
                      waits=emax(st["evsl"], self.ev_consts))
        evgb = self.v(lambda e: e.tensor_mul(self.Gsb, self.Gs, self.gnbB[:]), inc="dv")
        st["evgb"] = evgb

        def rot(psum, ci, si, outtm, dep):
            cosT = self.rope[:, ci, :]
            sinT = self.rope[:, si, :]
            pv = psum.rearrange("p (h half d) -> p h half d", h=8, half=2, d=32)
            sv = sinT.rearrange("p (h half d) -> p h half d", h=8, half=2, d=32)
            qbv = self.qb32.rearrange("p (h half d) -> p h half d", h=8, half=2, d=32)
            self.v(lambda e, psum=psum, cosT=cosT: e.tensor_mul(self.qa32, psum, cosT),
                   waits=emax(self.ev_rope, dep))
            self.v(lambda e, pv=pv, sv=sv, qbv=qbv: e.tensor_tensor(
                qbv[:, :, 0, :], pv[:, :, 1, :], sv[:, :, 0, :], OP.mult))
            self.v(lambda e, pv=pv, sv=sv, qbv=qbv: e.tensor_tensor(
                qbv[:, :, 1, :], pv[:, :, 0, :], sv[:, :, 1, :], OP.mult))
            return self.v(lambda e, outtm=outtm: e.tensor_add(outtm, self.qa32, self.qb32),
                          inc="dv")

        evq = rot(self.P0[:], 0, 1, self.qtm, st["evQ"])
        self.bank_free["P0"] = evq
        evk = rot(self.P1[:], 2, 3, self.ktm, st["evK"])
        self.bank_free["P1"] = evk
        st["evq"], st["evk"] = evq, evk
        self.ev_rope_read = evk
        self.tap("t_qtm", self.qtm, emax(evq))
        self.tap("t_ktm", self.ktm, emax(evk))
        nxt = ((s % SUB_PER_SEQ) * STOK + (tg + 1) * 128) % T
        self.ev_rope = self._load_rope(nxt, deps=emax(evk))

        self.v(lambda e: e.tensor_copy(self.v_tm[:, 0:512], self.P23[:, 0:512]),
               waits=emax(st["evVa"]))
        evv = self.v(lambda e: e.tensor_copy(self.v_tm[:, 512:1024], self.P23[:, 512:1024]),
                     waits=emax(st["evVb"]), inc="dv")
        self.bank_free["P23"] = evv
        st["evv"] = evv
        self.tap("t_vtm", self.v_tm, emax(evv))

        ev_tp2 = None
        for i in range(8):
            src = self.qtm if i < 4 else self.ktm
            cc = i % 4
            ev_tp2 = self.t(
                lambda e, src=src, cc=cc, i=i: e.transpose(
                    self.PT[:, i * 128:(i + 1) * 128],
                    src[:, cc * 128:(cc + 1) * 128], self.identb[:]),
                waits=emax(evq, evk, self.bank_free["PT"]) if i == 0 else (),
                inc="pe" if i == 7 else None)
        ev_fm = None
        for i in range(8):
            dst = self.QFM if i < 4 else self.KFM
            cc = i % 4
            ev_fm = self.v(
                lambda e, dst=dst, cc=cc, i=i: e.tensor_copy(
                    dst[:, cc * 128:(cc + 1) * 128], self.PT[:, i * 128:(i + 1) * 128]),
                waits=emax(ev_tp2) if i == 0 else (), inc="dv" if i == 7 else None)
        self.bank_free["PT"] = ev_fm
        st["ev_fm"] = ev_fm

    def _attn_back1a(self, s, tg, st):
        ev_fm = st["ev_fm"]
        evv = st["evv"]
        ev_sc = [None, None]
        for h in range(8):
            p = h // 2
            base = (h % 2) * 64
            psum = self.P0 if h % 2 == 0 else self.P1
            col = (h // 2) * 128
            ev = self.t(
                lambda e, p=p, base=base, psum=psum, col=col: e.matmul(
                    psum[:, col:col + 128],
                    self.KFM[base:base + 64, p * 128:(p + 1) * 128],
                    self.QFM[base:base + 64, p * 128:(p + 1) * 128],
                    start=True, stop=True),
                waits=emax(ev_fm,
                           self.bank_free["P0"] if h == 0 else None,
                           self.bank_free["P1"] if h == 1 else None),
                inc="pe" if h >= 6 else None)
            if h >= 6:
                ev_sc[h % 2] = ev

        self.v(lambda e: e.tensor_mul(self.scM[:, 0:512], self.P0[:], self.mask4[:]),
               waits=emax(ev_sc[0], self.ev_mask))
        evm = self.v(lambda e: e.tensor_mul(self.scM[:, 512:1024], self.P1[:], self.mask4[:]),
                     waits=emax(ev_sc[1]), inc="dv")
        self.bank_free["P0"] = evm
        self.bank_free["P1"] = evm
        self.tap("t_scM", self.scM, emax(evm))

        ev_o = None
        for h in range(8):
            p = h // 2
            col = h * 128
            smoff = (h % 2) * 512 + (h // 2) * 128
            self.t(lambda e, h=h, col=col, smoff=smoff: e.matmul(
                self.P23[:, col:col + 128],
                self.scM[:, smoff:smoff + 128],
                self.v_tm[:, h * 128:(h + 1) * 128],
                start=True, stop=False),
                waits=emax(evm, evv, self.bank_free["P23"]) if h == 0 else ())
            ev_o = self.t(lambda e, p=p, col=col, h=h: e.matmul(
                self.P23[:, col:col + 128],
                self.QFM[:, p * 128:(p + 1) * 128],
                self.spad[:, h * 128:(h + 1) * 128],
                start=False, stop=True),
                waits=emax(self.ev_spad) if h == 0 else (),
                inc="pe" if h == 7 else None)
        st["ev_o"] = ev_o

        self.v(lambda e: e.tensor_copy(self.o_sb[:, 0:512], self.P23[:, 0:512]),
               waits=emax(ev_o))
        ev_oev = self.v(lambda e: e.tensor_copy(self.o_sb[:, 512:1024], self.P23[:, 512:1024]),
                        inc="dv")
        self.bank_free["P23"] = ev_oev
        st["ev_oev"] = ev_oev

        ev_u = None
        for h in range(8):
            ev_u = self.t(lambda e, h=h: e.matmul(
                self.PS[:, h * 128:(h + 1) * 128],
                self.ktm[:, h * 64:(h + 1) * 64],
                self.v_tm[:, h * 128:(h + 1) * 128],
                start=True, stop=True),
                waits=emax(st["evsl"], st["evk"], evv) if h == 0 else (),
                inc="pe" if h == 7 else None)
        self._prev_u = ev_u

        self.v(lambda e: e.tensor_add(self.S32[:], self.S32[:], self.PS[:]),
               waits=emax(ev_u))
        evs2 = self.v(lambda e: e.tensor_mul(self.S32[:], self.S32[:], self.gCB[0:64, :]),
                      inc="dv")
        self.bank_free["P45"] = evs2
        spv = self.spad.rearrange("p (h v) -> p h v", h=8, v=128)
        sv32 = self.S32.rearrange("p (h v) -> p h v", h=8, v=128)
        self.v(lambda e, spv=spv, sv32=sv32: e.tensor_copy(spv[0:64, 0::2, :], sv32[:, 0::2, :]),
               waits=emax(ev_o))
        self.ev_spad = self.v(
            lambda e, spv=spv, sv32=sv32: e.tensor_copy(spv[64:128, 1::2, :], sv32[:, 1::2, :]),
            inc="dv")

    def _attn_gn_stats(self, s, tg, st):
        ev_st = None
        for h in range(8):
            col = h * 128
            self.stat_op(self.o_sb[:, col:col + 128], AF.Identity,
                         self.gst[:, h:h + 1],
                         extra_waits=emax(st["ev_oev"]) if h == 0 else ())
            ev_st = self.stat_op(self.o_sb[:, col:col + 128], AF.Square,
                                 self.gst[:, 8 + h:9 + h])
        st["ev_st"] = ev_st

    def _attn_gn_chain(self, s, tg, st):
        ev_st = st["ev_st"]
        gm = self.gmr[:, 0:8]
        gv = self.gmr[:, 8:16]
        gt = self.rs_t[:, 24:32]
        self.v(lambda e: e.tensor_scalar(gm, self.gst[:, 0:8], 1.0 / 128.0, None, OP.mult),
               waits=emax(ev_st))
        self.v(lambda e: e.tensor_scalar(gv, self.gst[:, 8:16], 1.0 / 128.0, None, OP.mult))
        self.v(lambda e: e.tensor_mul(gt, gm, gm))
        self.spacer()
        self.v(lambda e: e.scalar_tensor_tensor(gv, gt, -1.0, gv, OP.mult, OP.add))
        st["ev_rs"] = self.rsqrt(gv, gv, 8, so=0)

    def _attn_back2(self, s, tg, st, st_next):
        xs = self.xsb[s % 2]
        x_tg = st["x_tg"]
        gm = self.gmr[:, 0:8]
        gv = self.gmr[:, 8:16]
        ev_ap = None
        for h in range(8):
            col = h * 128
            ev_ap = self.v(lambda e, h=h, col=col: e.tensor_scalar(
                self.on_all[:, col:col + 128], self.o_sb[:, col:col + 128],
                gm[:, h:h + 1], gv[:, h:h + 1], OP.subtract, OP.mult),
                waits=emax(st["ev_rs"]) if h == 0 else (),
                inc="dv" if h == 7 else None)
        self.v(lambda e: e.tensor_mul(self.og, self.on_all, self.Gsw),
               waits=emax(st["evgb"]))
        ev_og = self.v(lambda e: e.tensor_add(self.og, self.og, self.Gsb), inc="dv")
        self.tap("t_og", self.og, emax(ev_og))

        ev_tp3 = None
        for c in range(8):
            ev_tp3 = self.t(
                lambda e, c=c: e.transpose(self.PT[:, c * 128:(c + 1) * 128],
                                           self.og[:, c * 128:(c + 1) * 128],
                                           self.identb[:]),
                waits=emax(ev_og, self.bank_free["PT"]) if c == 0 else (),
                inc="pe" if c == 7 else None)
        ev_ogfm = None
        for c in range(8):
            ev_ogfm = self.v(
                lambda e, c=c: e.tensor_copy(self.ogFM[:, c * 128:(c + 1) * 128],
                                             self.PT[:, c * 128:(c + 1) * 128]),
                waits=emax(ev_tp3) if c == 0 else (), inc="dv" if c == 7 else None)
        self.bank_free["PT"] = ev_ogfm
        self.tap("t_ogfm", self.ogFM, emax(ev_ogfm))

        ev_wo = None
        for cc in range(8):
            slot = cc % 4
            ev_dma = self.qa_(
                lambda e, cc=cc, slot=slot: e.dma_start(
                    out=self.small[slot], in_=self.wo_s[cc]),
                waits=emax(self.ev_wo_s, self.ev_small_free[slot]), inc="sm%d" % slot)
            if cc == 0:
                self.tap("t_wo0", self.small[0], emax(ev_dma))
            for half in range(2):
                ev_wo = self.t(lambda e, cc=cc, slot=slot, half=half: e.matmul(
                    self.P45[:, half * 512:(half + 1) * 512],
                    self.ogFM[:, cc * 128:(cc + 1) * 128],
                    self.small[slot][:, half * 512:(half + 1) * 512],
                    start=(cc == 0), stop=(cc == 7)),
                    waits=emax(ev_ogfm, ev_dma,
                               (st_next["evsl"] if st_next is not None
                                else self.bank_free["P45"]) if cc == 0 else None)
                    if half == 0 else (),
                    inc="pe" if half == 1 else None)
            self.ev_small_free[slot] = ev_wo

        ev_res = self.v(lambda e, x_tg=x_tg: e.tensor_add(x_tg, x_tg, self.P45[:]),
                        waits=emax(ev_wo), inc="dv")
        self.bank_free["P45"] = ev_res
        if tg == 3:
            self.tap("t_x1", xs[:, 0, :], emax(ev_res))

        self.stat_op(x_tg, AF.Identity, self.lst2[:, tg * 2:tg * 2 + 1],
                     extra_waits=emax(ev_res, self.ev_lst2_free))
        ev_l2 = self.stat_op(x_tg, AF.Square, self.lst2[:, tg * 2 + 1:tg * 2 + 2])
        self.ev_ln2_stats = ev_l2
        self.ev_attn_done = ev_res

    def _attn_phase(self, s):
        self._atn = None
        st = self._attn_front(s, 0)
        self._atn = st
        self._attn_mid_a(s, 0, st)
        self._attn_mid_b(s, 0, st)
        for tg in range(4):
            st = self._atn
            self._attn_back1a(s, tg, st)
            self._attn_gn_stats(s, tg, st)
            nst = None
            if tg < 3:
                nst = self._attn_front(s, tg + 1)
            self._attn_gn_chain(s, tg, st)
            if tg < 3:
                self._attn_mid_a(s, tg + 1, nst)
            self._attn_back2(s, tg, st, nst)
            if tg < 3:
                self._atn = nst
                self._attn_mid_b(s, tg + 1, nst)


    def _ln_chain(self, lst, mdst, rdst, ev_stats, so):
        lv = lst[:, 0:8].rearrange("p (tg c) -> p tg c", tg=4, c=2)
        t4 = self.rs_t[:, so + 8:so + 12]
        b4 = self.rs_b[:, so + 8:so + 12]
        self.v(lambda e: e.tensor_scalar(mdst, lv[:, :, 0], 1.0 / 1024.0, None, OP.mult),
               waits=emax(ev_stats, self.ev_consts))
        self.v(lambda e: e.tensor_scalar(b4, lv[:, :, 1], 1.0 / 1024.0, None, OP.mult))
        self.v(lambda e: e.tensor_mul(t4, mdst, mdst))
        self.spacer()
        self.v(lambda e: e.scalar_tensor_tensor(b4, t4, -1.0, b4, OP.mult, OP.add))
        return self.rsqrt(rdst, b4, 4, so=so)

    def _ln1_block(self, s):
        xs = self.xsb[s % 2]
        ev_stats = None
        for tg in range(4):
            x_tg = xs[:, tg, :]
            self.stat_op(x_tg, AF.Identity, self.lst[:, tg * 2:tg * 2 + 1],
                         extra_waits=emax(self.ev_xload[s], self.ev_lst_free) if tg == 0 else ())
            ev_stats = self.stat_op(x_tg, AF.Square, self.lst[:, tg * 2 + 1:tg * 2 + 2])
        ev = self._ln_chain(self.lst, self.lnm[:, 0:4], self.lnm[:, 4:8],
                            ev_stats, so=0)
        self.ev_lst_free = ev
        return ev

    def _subtile(self, s):
        xs = self.xsb[s % 2]
        if s % SUB_PER_SEQ == 0 and s > 0:
            self.v(lambda e: e.memset(self.S32[:], 0.0))
            self.ev_spad = self.v(lambda e: e.memset(self.spad[:], 0.0), inc="dv")

        if self.pending_ln1.get(s) is None:
            self.pending_ln1[s] = self._ln1_block(s)
        self.ev_ln1_chain = self.pending_ln1[s]

        if BUILD_PHASE == "ln1":
            ev_store = self.gd(lambda e, s=s: e.dma_start(out=self.outv[s], in_=xs[:]),
                               waits=emax(self.ev_ln1_chain), inc="st%d" % (s % 2))
            if s + 2 < BUILD_NSUB:
                self.ev_xload[s + 2] = self.qa_(
                    lambda e, s=s: e.dma_start(out=self.xsb[s % 2][:], in_=self.xv[s + 2]),
                    waits=emax(ev_store), inc="xl%d" % (s % 2))
            return

        self._attn_phase(s)

        if BUILD_PHASE == "attn":
            ev_store = self.gd(lambda e, s=s: e.dma_start(out=self.outv[s], in_=xs[:]),
                               waits=emax(self.ev_attn_done), inc="st%d" % (s % 2))
            if s + 2 < BUILD_NSUB:
                self.ev_xload[s + 2] = self.qa_(
                    lambda e, s=s: e.dma_start(out=self.xsb[s % 2][:], in_=self.xv[s + 2]),
                    waits=emax(ev_store), inc="xl%d" % (s % 2))
            self.ev_arena_free = self.ev_attn_done
            return

        ev_ln2_chain = self._ln_chain(self.lst2, self.lnm[:, 8:12], self.lnm[:, 12:16],
                                      self.ev_ln2_stats, so=4)
        self.ev_lst2_free = ev_ln2_chain
        ev_x2 = None
        for tg in range(4):
            x_tg = xs[:, tg, :]
            m = self.lnm[:, 8 + tg:9 + tg]
            r = self.lnm[:, 12 + tg:13 + tg]
            ev_ap = self.v(lambda e, x_tg=x_tg, m=m, r=r: e.tensor_scalar(
                self.xln_tm, x_tg, m, r, OP.subtract, OP.mult),
                waits=emax(ev_ln2_chain), inc="dv")
            ev_tp = None
            for c in range(8):
                ev_tp = self.t(lambda e, c=c: e.transpose(
                    self.PT[:, c * 128:(c + 1) * 128],
                    self.xln_tm[:, c * 128:(c + 1) * 128], self.identb[:]),
                    waits=emax(ev_ap, self.bank_free["PT"]) if c == 0 else (),
                    inc="pe" if c == 7 else None)
            for c in range(8):
                ev_x2 = self.v(lambda e, c=c, tg=tg: e.tensor_copy(
                    self.x2FM[:, c, tg * 128:(tg + 1) * 128],
                    self.PT[:, c * 128:(c + 1) * 128]),
                    waits=emax(ev_tp, self.ev_x2fm_free if (tg == 0 and c == 0) else None)
                    if c == 0 else (),
                    inc="dv" if c == 7 else None)
            self.bank_free["PT"] = ev_x2
        self.tap("t_x2fm", self.x2FM[:, 0:4, :], emax(ev_x2))
        ev_arena = emax(self.ev_attn_done, ev_x2)

        pb_even = (self.P0[:], "P0", self.P1[:], "P1")
        pb_odd = (self.P23[:, 0:512], "P23", self.P23[:, 512:1024], "P23")
        ev_y = None
        ev_y1mm_last = None
        for oc in range(32):
            sl1 = oc % 2
            sl2 = 2 + oc % 2
            evd1 = self.q(lambda e, oc=oc, sl=sl1: e.dma_start(
                out=self.small_iv[sl], in_=self.w1_s[:, oc]),
                waits=emax(self.ev_w1_s, self.ev_small_free[sl1]), inc="sm%d" % sl1)
            evd2 = self.q(lambda e, oc=oc, sl=sl2: e.dma_start(
                out=self.small_iv[sl], in_=self.w2_s[:, oc]),
                waits=emax(self.ev_w2_s, self.ev_small_free[sl2]), inc="sm%d" % sl2)
            pa, ka, pb, kb = pb_even if oc % 2 == 0 else pb_odd
            ev_m1 = None
            for ic in range(8):
                ev_m1 = self.t(lambda e, ic=ic, pa=pa, sl=sl1: e.matmul(
                    pa, self.small_iv[sl][:, ic, :], self.x2FM[:, ic, :],
                    start=(ic == 0), stop=(ic == 7)),
                    waits=emax(evd1, ev_x2, self.bank_free[ka]) if ic == 0 else (),
                    inc="pe" if ic == 7 else None)
            ev_m2 = None
            for ic in range(8):
                ev_m2 = self.t(lambda e, ic=ic, pb=pb, sl=sl2: e.matmul(
                    pb, self.small_iv[sl][:, ic, :], self.x2FM[:, ic, :],
                    start=(ic == 0), stop=(ic == 7)),
                    waits=emax(evd2, self.bank_free[kb]) if ic == 0 else (),
                    inc="pe" if ic == 7 else None)
            ev_y1mm_last = ev_m2
            self.ev_small_free[sl1] = ev_m1
            self.ev_small_free[sl2] = ev_m2
            asl = oc % 2
            ev_a = self.a(lambda e, oc=oc, pa=pa, asl=asl: e.activation(
                self.abf[:, asl, :], pa, AF.Silu, scale=self.s1[:, oc:oc + 1]),
                waits=emax(ev_m1, self.ev_abf_free[asl]), inc="ac")
            ev_y = self.v(lambda e, oc=oc, pb=pb, asl=asl: e.scalar_tensor_tensor(
                self.y1[:, oc, :], pb, self.s2[:, oc:oc + 1], self.abf[:, asl, :],
                OP.mult, OP.mult),
                waits=emax(ev_m2, ev_a, *(ev_arena if oc == 0 else [])), inc="dv")
            self.ev_abf_free[asl] = ev_y
            self.bank_free[ka] = ev_a
            self.bank_free[kb] = ev_y
        ev_y1_all = ev_y
        self.tap("t_y1", self.y1[:, 0, :], emax(ev_y1_all))

        def h_stage(scr, scr_evs, src_view, dst_store, s_ap, src_ready, dst_free,
                    fp32_mode=False, fp32_src=None, cast_to=None, cast_sem=None):
            ev_g = None
            p45_free = [self.bank_free["P45"], self.bank_free["P45"]]
            for oc in range(32):
                slot = oc % 2
                if fp32_mode:
                    evd = self.gd(lambda e, oc=oc, slot=slot, scr=fp32_src: e.dma_start(
                        out=self.hslab[slot], in_=scr[:, oc]),
                        waits=emax(self.ev_hslab_free[slot]), inc="hf%d" % slot)
                    if cast_to is not None:
                        self.gd(lambda e, oc=oc: e.dma_start(
                            out=cast_to[0][:, oc], in_=cast_to[1][:, oc]), inc=cast_sem)
                else:
                    evd = self.q(lambda e, oc=oc, slot=slot, scr=scr: e.dma_start(
                        out=self.hslab[slot], in_=scr[:, oc]),
                        waits=emax(self.ev_hslab_free[slot], scr_evs[oc // 8]), inc="hs%d" % slot)
                psum = self.P45[:, 0:512] if oc % 2 == 0 else self.P45[:, 512:1024]
                ev_m = None
                for ic in range(32):
                    ev_m = self.t(lambda e, ic=ic, slot=slot, psum=psum,
                                  src_view=src_view: e.matmul(
                        psum, self.hslab[slot][:, ic, :], src_view[:, ic, :],
                        start=(ic == 0), stop=(ic == 31)),
                        waits=emax(evd, src_ready, p45_free[oc % 2]) if ic == 0 else (),
                        inc="pe" if ic == 31 else None)
                self.ev_hslab_free[slot] = ev_m
                ev_g = self.a(lambda e, oc=oc, psum=psum, s_ap=s_ap,
                              dst_store=dst_store: e.activation(
                    dst_store[:, oc, :], psum, AF.Gelu, scale=s_ap[:, oc:oc + 1]),
                    waits=emax(ev_m, dst_free if oc == 0 else None), inc="ac")
                p45_free[oc % 2] = ev_g
            self.bank_free["P45"] = ev_g
            return ev_g

        if s <= 1:
            ev_h1 = h_stage(self.h1_s, self.ev_h1_s, self.y1, self.hb, self.s1h,
                            ev_y1_all, self.ev_hb_free, fp32_mode=True,
                            fp32_src=self.h1_d,
                            cast_to=(self.h1_s, self.h1_d) if s == 0 else None,
                            cast_sem="suc")
        else:
            ev_h1 = h_stage(self.h1_s, self.ev_h1_s, self.y1, self.hb, self.s1h,
                            ev_y1_all, self.ev_hb_free)
        self.tap("t_hb", self.hb[:, 0, :], emax(ev_h1))
        if s <= 1:
            ev_h2 = h_stage(self.h2_s, self.ev_h2_s, self.hb, self.y1, self.s2h,
                            ev_h1, None, fp32_mode=True, fp32_src=self.h2_d,
                            cast_to=(self.h2_s, self.h2_d) if s == 0 else None,
                            cast_sem="sud")
        else:
            ev_h2 = h_stage(self.h2_s, self.ev_h2_s, self.hb, self.y1, self.s2h,
                            ev_h1, None)
        self.ev_hb_free = ev_h2
        self.ev_x2fm_free = ev_h2

        if s + 1 < BUILD_NSUB:
            self.pending_ln1[s + 1] = self._ln1_block(s + 1)

        w3psum = [(self.P0[:], "P0"), (self.P1[:], "P1"),
                  (self.P23[:, 0:512], "P23"), (self.P23[:, 512:1024], "P23")]
        ev_evac = None
        for half in range(2):
            ev_mm = [None] * 4
            for oc in range(32):
                sl = oc % 4
                evd = self.q(lambda e, oc=oc, sl=sl, half=half: e.dma_start(
                    out=self.small[sl][:, 0:512],
                    in_=self.w3_s[:, oc, half * 512:(half + 1) * 512]),
                    waits=emax(self.ev_w3_s, self.ev_small_free[sl]), inc="sm%d" % sl)
                for tg in range(4):
                    psum, key = w3psum[tg]
                    ev_mm[tg] = self.t(lambda e, oc=oc, tg=tg, sl=sl, psum=psum: e.matmul(
                        psum, self.y1[:, oc, tg * 128:(tg + 1) * 128],
                        self.small[sl][:, 0:512],
                        start=(oc == 0), stop=(oc == 31)),
                        waits=emax(evd, ev_h2,
                                   self.bank_free[key] if oc == 0 else None)
                        if oc == 0 else emax(evd),
                        inc="pe" if tg == 3 else None)
                self.ev_small_free[sl] = ev_mm[3]
            for tg in range(4):
                psum, key = w3psum[tg]
                x_sl = xs[:, tg, half * 512:(half + 1) * 512]
                self.v(lambda e, psum=psum, half=half: e.tensor_mul(
                    self.w3scr[:, 0:512], psum, self.s3B[:, half * 512:(half + 1) * 512]),
                    waits=emax(ev_mm[3]))
                ev_evac = self.v(lambda e, x_sl=x_sl: e.tensor_add(
                    x_sl, x_sl, self.w3scr[:, 0:512]), inc="dv")
                self.bank_free[key] = ev_evac
            for sl in range(4):
                self.ev_small_free[sl] = ev_mm[3]

        if s == 0:
            self.ev_h1_s = [("suc", self.s.count["suc"])] * 4
            self.ev_h2_s = [("sud", self.s.count["sud"])] * 4
        ev_store = self.gd(lambda e, s=s: e.dma_start(out=self.outv[s], in_=xs[:]),
                          waits=emax(ev_evac), inc="st%d" % (s % 2))
        if s + 2 < NSUB:
            self.ev_xload[s + 2] = self.qa_(
                lambda e, s=s: e.dma_start(out=self.xsb[s % 2][:], in_=self.xv[s + 2]),
                waits=emax(ev_store), inc="xl%d" % (s % 2))
        self.ev_arena_free = ev_mm[3]

    def _replay(self):
        nc = self.nc
        import contextlib
        with contextlib.ExitStack() as es:
            semh = {n: es.enter_context(nc.semaphore(n)) for n in self.s.count}
            block = es.enter_context(nc.Block())

            def mk(eng_ops):
                def run(eng):
                    for ws, fn, inc, inc_n in eng_ops:
                        for (sname, val) in ws:
                            eng.wait_ge(semh[sname], val)
                        inst = fn(eng)
                        if inc is not None:
                            inst.then_inc(semh[inc], inc_n)
                return run

            block.tensor(mk(self.s.ops["tensor"]))
            block.vector(mk(self.s.ops["vector"]))
            block.scalar(mk(self.s.ops["scalar"]))
            block.sync(mk(self.s.ops["sync"]))
            block.gpsimd(mk(self.s.ops["gpsimd"]))


_CACHE = {}


def build_program():
    if "nc" not in _CACHE:
        b = Builder()
        _CACHE["nc"] = b.build()
    return _CACHE["nc"]


def kernel(**inputs):
    nc = build_program()
    prep = host_prepare(inputs)
    x = np.ascontiguousarray(inputs["L0_attn_x"].reshape(B, T, HID), np.float32)
    in_maps = []
    for c in range(NCORES):
        m = {k: v for k, v in prep.items()}
        m["x"] = np.ascontiguousarray(x[2 * c:2 * c + 2].reshape(TOK, HID))
        in_maps.append(m)
    res = run_bass_kernel_spmd(nc, in_maps, core_ids=list(range(NCORES)))
    out = np.empty((B, T, HID), np.float32)
    for c in range(NCORES):
        out[2 * c:2 * c + 2] = res.results[c]["out"].reshape(2, T, HID)
    return out


# version 07
# speedup vs baseline: 1.1794x; 1.1794x over previous
import math
import numpy as np

import concourse.bass as bass
import concourse.mybir as mybir
from concourse.bass_utils import run_bass_kernel_spmd
from concourse.alu_op_type import AluOpType as OP

AF = mybir.ActivationFunctionType
F32 = mybir.dt.float32
BF16 = mybir.dt.bfloat16

NCORES = 8
B, T, HID = 16, 2048, 1024
H, D, V = 8, 64, 128
DH = 4096
TOK = (B // NCORES) * T
STOK = 512
NSUB = TOK // STOK
SUB_PER_SEQ = T // STOK
EPS = 1e-5

ENGINES = ("tensor", "vector", "scalar", "sync", "gpsimd")

RSQRT_TANGENTS = [((1.0 / (2.0 * math.sqrt(a))),
                   (math.sqrt(a) / 2.0) + EPS / (2.0 * math.sqrt(a)))
                  for a in (10.0 ** e for e in range(-5, 5))]

DEBUG_TAPS = False
BUILD_NSUB = NSUB
BUILD_PHASE = "full"
BUILD_ATTN_STAGES = 99


class Sched:

    def __init__(self):
        self.ops = {e: [] for e in ENGINES}
        self.count = {}
        self.last_wait = {e: {} for e in ENGINES}

    def sem(self, name):
        self.count.setdefault(name, 0)

    def emit(self, eng, fn, waits=(), inc=None, inc_n=1):
        ws = []
        for w in waits:
            if w is None:
                continue
            s, v = w
            if self.last_wait[eng].get(s, -1) < v:
                ws.append((s, v))
                self.last_wait[eng][s] = v
        ev = None
        if inc is not None:
            self.count[inc] += inc_n
            ev = (inc, self.count[inc])
        self.ops[eng].append((ws, fn, inc, inc_n))
        return ev


def emax(*evs):
    return [e for e in evs if e is not None]


def host_prepare(inputs):
    f = np.float32
    d = {}
    Wq = inputs["L0_attn_Wq"]; Wk = inputs["L0_attn_Wk"]
    Wv = inputs["L0_attn_Wv"]; Wg = inputs["L0_attn_Wg"]
    Wo = inputs["L0_attn_Wo"]
    d["wq_l"] = np.ascontiguousarray(Wq.reshape(512, 8, 128).transpose(2, 1, 0))
    d["wk_l"] = np.ascontiguousarray(Wk.reshape(512, 8, 128).transpose(2, 1, 0))
    d["wv_l"] = np.ascontiguousarray(Wv.reshape(1024, 8, 128).transpose(2, 1, 0))
    d["wg_l"] = np.ascontiguousarray(Wg.reshape(1024, 8, 128).transpose(2, 1, 0))
    d["wo_l"] = np.ascontiguousarray(Wo.reshape(1024, 8, 128).transpose(1, 2, 0))
    W1 = inputs["L0_ffn_w1_w"]; W2 = inputs["L0_ffn_w2_w"]
    H1 = inputs["L0_ffn_h1_w"]; H2 = inputs["L0_ffn_h2_w"]
    W3 = inputs["L0_ffn_w3_w"]
    d["w1_l"] = np.ascontiguousarray(W1.reshape(32, 128, 8, 128).transpose(3, 0, 2, 1))
    d["w2_l"] = np.ascontiguousarray(W2.reshape(32, 128, 8, 128).transpose(3, 0, 2, 1))
    d["h1_l"] = np.ascontiguousarray(H1.reshape(32, 128, 32, 128).transpose(3, 0, 2, 1))
    d["h2_l"] = np.ascontiguousarray(H2.reshape(32, 128, 32, 128).transpose(3, 0, 2, 1))
    d["w3_l"] = np.ascontiguousarray(W3.reshape(1024, 32, 128).transpose(2, 1, 0))
    d["s1_l"] = np.ascontiguousarray(inputs["L0_ffn_w1_s"].reshape(32, 128).T)
    d["s2_l"] = np.ascontiguousarray(inputs["L0_ffn_w2_s"].reshape(32, 128).T)
    d["s1h_l"] = np.ascontiguousarray(inputs["L0_ffn_h1_s"].reshape(32, 128).T)
    d["s2h_l"] = np.ascontiguousarray(inputs["L0_ffn_h2_s"].reshape(32, 128).T)
    d["s3B"] = np.ascontiguousarray(np.tile(inputs["L0_ffn_w3_s"][None, :], (128, 1)))
    d["gnwB"] = np.ascontiguousarray(np.tile(inputs["L0_attn_gn_weight"][None, :], (128, 1)))
    d["gnbB"] = np.ascontiguousarray(np.tile(inputs["L0_attn_gn_bias"][None, :], (128, 1)))

    inv32 = (1.0 / 10000.0 ** (np.arange(0, D, 2, dtype=np.float64) / D)).astype(f)
    tpos = np.arange(T, dtype=f)
    fr = (tpos[:, None] * inv32[None, :]).astype(f)
    cos32 = np.cos(fr).astype(np.float64)
    sin32 = np.sin(fr).astype(np.float64)
    gamma = 1.0 - 2.0 ** (-5.0 - np.arange(H, dtype=np.float64))
    j = (np.arange(T) % 128).astype(np.float64)
    gq = gamma[None, :] ** (j[:, None] + 1) * D ** -0.5
    gk = gamma[None, :] ** (-(j[:, None] + 1))

    def rope_tables(gscale):
        cosT = np.empty((T, H, 2, 32), np.float64)
        sinT = np.empty((T, H, 2, 32), np.float64)
        for h in range(H):
            cosT[:, h, 0, :] = cos32 * gscale[:, h:h + 1]
            cosT[:, h, 1, :] = cos32 * gscale[:, h:h + 1]
            sinT[:, h, 0, :] = -sin32 * gscale[:, h:h + 1]
            sinT[:, h, 1, :] = sin32 * gscale[:, h:h + 1]
        return (np.ascontiguousarray(cosT.reshape(T, 512).astype(f)),
                np.ascontiguousarray(sinT.reshape(T, 512).astype(f)))

    d["ropeQc"], d["ropeQs"] = rope_tables(gq)
    d["ropeKc"], d["ropeKs"] = rope_tables(gk)
    m = (np.arange(128)[None, :] >= np.arange(128)[:, None]).astype(f)
    d["mask4"] = np.ascontiguousarray(np.tile(m, (1, 4)))
    gC = (gamma ** 128).astype(f)
    d["gCB"] = np.ascontiguousarray(np.tile(np.repeat(gC, 128)[None, :], (128, 1)))
    d["ident"] = np.eye(128, dtype=f)
    cp = np.zeros((128, 8), f)
    cp[:, 0] = EPS
    cp[:, 1] = 1.0 / 1024.0
    d["cpack"] = cp
    return d




class Builder:
    def __init__(self):
        self.nc = bass.Bass("TRN2", target_bir_lowering=False)
        self.s = Sched()
        for name in ("pe", "dv", "ac", "po", "cst", "sua", "sub", "suc", "sud", "suw", "hf0", "hf1",
                     "xl0", "xl1", "st0", "st1", "rp",
                     "sm0", "sm1", "sm2", "sm3", "hs0", "hs1"):
            self.s.sem(name)
        self._declare_io()
        self._alloc()

    def _declare_io(self):
        nc = self.nc

        def dt(n, sh, kind="ExternalInput"):
            return nc.dram_tensor(n, sh, F32, kind=kind).ap()

        self.x_in = dt("x", [TOK, HID])
        self.out = dt("out", [TOK, HID], kind="ExternalOutput")
        self.wq_d = dt("wq_l", [128, 8, 512])
        self.wk_d = dt("wk_l", [128, 8, 512])
        self.wv_d = dt("wv_l", [128, 8, 1024])
        self.wg_d = dt("wg_l", [128, 8, 1024])
        self.wo_d = dt("wo_l", [8, 128, 1024])
        self.w1_d = dt("w1_l", [128, 32, 8, 128])
        self.w2_d = dt("w2_l", [128, 32, 8, 128])
        self.h1_d = dt("h1_l", [128, 32, 32, 128])
        self.h2_d = dt("h2_l", [128, 32, 32, 128])
        self.w3_d = dt("w3_l", [128, 32, 1024])
        self.s1_d = dt("s1_l", [128, 32])
        self.s2_d = dt("s2_l", [128, 32])
        self.s1h_d = dt("s1h_l", [128, 32])
        self.s2h_d = dt("s2h_l", [128, 32])
        self.s3B_d = dt("s3B", [128, 1024])
        self.gnwB_d = dt("gnwB", [128, 1024])
        self.gnbB_d = dt("gnbB", [128, 1024])
        self.ropeQc_d = dt("ropeQc", [T, 512])
        self.ropeQs_d = dt("ropeQs", [T, 512])
        self.ropeKc_d = dt("ropeKc", [T, 512])
        self.ropeKs_d = dt("ropeKs", [T, 512])
        self.mask4_d = dt("mask4", [128, 512])
        self.gCB_d = dt("gCB", [128, 1024])
        self.ident_d = dt("ident", [128, 128])
        self.cpack_d = dt("cpack", [128, 8])

        def dsc(n, sh):
            return nc.dram_tensor(n, sh, BF16, kind="Internal").ap()

        self.wo_s = dsc("wo_scr", [8, 128, 1024])
        self.w1_s = dsc("w1_scr", [128, 32, 8, 128])
        self.w2_s = dsc("w2_scr", [128, 32, 8, 128])
        self.h1_s = dsc("h1_scr", [128, 32, 32, 128])
        self.h2_s = dsc("h2_scr", [128, 32, 32, 128])
        self.w3_s = dsc("w3_scr", [128, 32, 1024])
        self.tap_d = {}
        if DEBUG_TAPS:
            for name, sh, ty in [("t_xln", [128, 1024], BF16), ("t_qtm", [128, 512], BF16),
                             ("t_ktm", [128, 512], BF16), ("t_vtm", [128, 1024], BF16),
                             ("t_scM", [128, 1024], BF16), ("t_og", [128, 1024], BF16),
                             ("t_x1", [128, 1024], F32), ("t_x2fm", [128, 2048], BF16),
                             ("t_y1", [128, 512], BF16), ("t_hb", [128, 512], BF16),
                             ("t_ogfm", [128, 1024], BF16), ("t_wo0", [128, 1024], BF16)]:
                self.tap_d[name] = nc.dram_tensor(name, sh, ty, kind="ExternalOutput").ap()

    def _alloc(self):
        nc = self.nc

        def sb(n, sh, ty):
            return nc.alloc_sbuf_tensor(n, sh, ty).ap()

        self.wq = sb("wq", [128, 8, 512], BF16)
        self.wk = sb("wk", [128, 8, 512], BF16)
        self.wv = sb("wv", [128, 8, 1024], BF16)
        self.wg = sb("wg", [128, 8, 1024], BF16)
        self.xsb = [sb("x_a", [128, 4, 1024], F32), sb("x_b", [128, 4, 1024], F32)]
        self.gnwB = sb("gnwB_s", [128, 1024], F32)
        self.gnbB = sb("gnbB_s", [128, 1024], F32)
        self.s3B = sb("s3B_s", [128, 1024], F32)
        self.gCB = sb("gCB_s", [128, 1024], F32)
        self.mask4 = sb("mask4_s", [128, 512], BF16)
        self.s1 = sb("s1_s", [128, 32], F32)
        self.s2 = sb("s2_s", [128, 32], F32)
        self.s1h = sb("s1h_s", [128, 32], F32)
        self.s2h = sb("s2h_s", [128, 32], F32)
        self.cpack = sb("cpack_s", [128, 8], F32)
        self.identb = sb("identb", [128, 128], BF16)
        self.rope = sb("rope", [128, 4, 512], F32)
        self.S32 = sb("S32", [64, 1024], F32)
        self.spad = sb("spad", [128, 1024], BF16)
        self.arena = sb("arena", [128, 16384], BF16)
        a = self.arena
        self.xln_tm = a[:, 0:1024]
        self.xlnFM = a[:, 1024:2048]
        self.qtm = a[:, 2048:2560]
        self.ktm = a[:, 2560:3072]
        self.QFM = a[:, 3072:3584]
        self.KFM = a[:, 3584:4096]
        self.v_tm = a[:, 4096:5120]
        self.Gs = a[:, 5120:6144]
        self.Gsw = a[:, 6144:7168]
        self.Gsb = a[:, 7168:8192]
        self.scM = a[:, 8192:9216]
        self.on_all = a[:, 9216:10240]
        self.og = a[:, 10240:11264]
        self.ogFM = a[:, 11264:12288]
        self.qa32 = a[:, 12288:13312].bitcast(F32)
        self.qb32 = a[:, 13312:14336].bitcast(F32)
        self.o_sb = a[:, 14336:16384].bitcast(F32)
        self.y1 = a.rearrange("p (o t) -> p o t", o=32, t=512)
        self.hb = sb("hb", [128, 32, 512], BF16)
        self.x2FM = self.hb[:, 0:8, :]
        self.wbuf = sb("wbuf", [128, 12288], BF16)
        self.hslab = [self.wbuf[:, 0:4096].rearrange("p (i o) -> p i o", i=32, o=128),
                      self.wbuf[:, 4096:8192].rearrange("p (i o) -> p i o", i=32, o=128)]
        self.small = [self.wbuf[:, 8192 + i * 1024: 8192 + (i + 1) * 1024] for i in range(4)]
        self.small_iv = [s.rearrange("p (i o) -> p i o", i=8, o=128) for s in self.small]
        self.abf = sb("abf", [128, 2, 512], BF16)
        self.w3scr = self.abf.rearrange("p a b -> p (a b)")[:, 0:1024].bitcast(F32)
        self.dumps = [sb("dump0", [128, 1024], BF16), sb("dump1", [128, 1024], BF16)]
        self._dump_i = 0
        self._dump_ev = [None, None]
        self.lst = sb("lst", [128, 16], F32)
        self.lst2 = sb("lst2", [128, 16], F32)
        self.lnm = sb("lnm", [128, 16], F32)
        self.gst = sb("gst", [128, 16], F32)
        self.gmr = sb("gmr", [128, 16], F32)
        self.rs_s = sb("rs_s", [128, 64], F32)
        self.rs_b = sb("rs_b", [128, 64], F32)
        self.rs_v = sb("rs_v", [128, 64], F32)
        self.rs_t = sb("rs_t", [128, 64], F32)
        self.junk = sb("junk", [128, 64], F32)

        def ps(n, sh, ty):
            return nc.alloc_psum_tensor(n, sh, ty).ap()

        self.P0 = ps("P0", [128, 512], F32)
        self.P1 = ps("P1", [128, 512], F32)
        self.P23 = ps("P23", [128, 1024], F32)
        self.P45 = ps("P45", [128, 1024], F32)
        self.PT = ps("PT", [128, 1024], BF16)
        self.PS = self.P45[0:64, :]

    _tap_pending = None

    def _tw(self, waits):
        if self._tap_pending is None:
            return waits
        return list(waits) + [self._tap_pending]

    def t(self, fn, waits=(), inc=None):
        return self.s.emit("tensor", fn, self._tw(waits), inc)

    def v(self, fn, waits=(), inc=None):
        return self.s.emit("vector", fn, self._tw(waits), inc)

    def a(self, fn, waits=(), inc=None):
        return self.s.emit("scalar", fn, self._tw(waits), inc)

    def g(self, fn, waits=(), inc=None, inc_n=1):
        return self.s.emit("gpsimd", fn, waits, inc, inc_n)

    def gd(self, fn, waits=(), inc=None, inc_n=16):
        return self.s.emit("gpsimd", fn, waits, inc, inc_n)

    def q(self, fn, waits=(), inc=None, inc_n=16):
        return self.s.emit("sync", fn, waits, inc, inc_n)

    def qa_(self, fn, waits=(), inc=None, inc_n=16):
        return self.s.emit("scalar", fn, waits, inc, inc_n)

    def tap(self, name, src_ap, waits):
        if DEBUG_TAPS and name in self.tap_d:
            out_ap = self.tap_d.pop(name)
            ev = self.gd(lambda e, o=out_ap, s=src_ap: e.dma_start(out=o, in_=s),
                         waits=waits, inc="st0")
            self._tap_pending = ev

    def spacer(self):
        self.v(lambda e: e.tensor_scalar(self.junk[:], self.junk[:], 0.0, None, OP.mult))

    def rsqrt(self, out_ap, v_ap, n, so=0):
        sA = self.rs_s[:, so:so + n]
        sB = self.rs_s[:, so + n:so + 2 * n]
        tA = self.rs_t[:, so:so + n]
        tB = self.rs_t[:, so + n:so + 2 * n]
        bA = self.rs_b[:, so:so + n]
        bB = self.rs_b[:, so + n:so + 2 * n]
        vh = self.rs_v[:, so:so + n]
        self.spacer()
        T = RSQRT_TANGENTS
        self.v(lambda e: e.tensor_scalar(sA, v_ap, T[0][0], T[0][1], OP.mult, OP.add))
        self.v(lambda e: e.tensor_scalar(sB, v_ap, T[1][0], T[1][1], OP.mult, OP.add))
        for i in range(2, len(T), 2):
            c1a, c2a = T[i]
            c1b, c2b = T[i + 1]
            self.v(lambda e, c1a=c1a, c2a=c2a: e.tensor_scalar(tA, v_ap, c1a, c2a, OP.mult, OP.add))
            self.v(lambda e, c1b=c1b, c2b=c2b: e.tensor_scalar(tB, v_ap, c1b, c2b, OP.mult, OP.add))
            self.v(lambda e: e.tensor_tensor(sA, sA, tA, OP.min))
            self.v(lambda e: e.tensor_tensor(sB, sB, tB, OP.min))
        self.v(lambda e: e.tensor_scalar(vh, v_ap, 0.5, 0.5 * EPS, OP.mult, OP.add))
        self.v(lambda e: e.tensor_tensor(sA, sA, sB, OP.min))
        self.spacer()
        self.v(lambda e: e.reciprocal(sA, sA))
        nit = 4
        for i in range(nit):
            self.spacer()
            self.v(lambda e: e.tensor_mul(bA, sA, sA))
            self.spacer()
            self.v(lambda e: e.tensor_mul(bB, bA, vh))
            self.spacer()
            if i == nit - 1:
                return self.v(lambda e: e.scalar_tensor_tensor(
                    out_ap, bB, 1.5, sA, OP.subtract, OP.mult), inc="dv")
            self.v(lambda e: e.scalar_tensor_tensor(sA, bB, 1.5, sA, OP.subtract, OP.mult))

    def stat_op(self, in_ap, func, acc_ap, extra_waits=()):
        i = self._dump_i % 2
        self._dump_i += 1
        dmp = self.dumps[i][:, 0:in_ap.shape[-1]]
        ev = self.a(lambda e: e.activation(dmp, in_ap, func, accum_out=acc_ap),
                    waits=emax(self._dump_ev[i], *extra_waits), inc="ac")
        self._dump_ev[i] = ev
        return ev

    def build(self):
        self._setup()
        for s in range(BUILD_NSUB):
            self._subtile(s)
        dma_sems = ("st0", "st1", "xl0", "xl1", "rp", "sm0", "sm1", "sm2",
                    "sm3", "hs0", "hs1", "hf0", "hf1", "sua", "sub", "suc", "sud",
                    "suw", "cst")
        self.g(lambda e: e.nop(),
               waits=[(s, self.s.count[s]) for s in dma_sems])
        self._replay()
        return self.nc

    def _setup(self):
        ev = None
        for dst, src in [
            (self.gnwB, self.gnwB_d), (self.gnbB, self.gnbB_d),
            (self.s3B, self.s3B_d), (self.gCB, self.gCB_d),
            (self.s1, self.s1_d),
            (self.s2, self.s2_d), (self.s1h, self.s1h_d),
            (self.s2h, self.s2h_d), (self.cpack, self.cpack_d),
        ]:
            ev = self.q(lambda e, d=dst, s=src: e.dma_start(out=d, in_=s), inc="cst")
        self.ev_consts = ev
        self.gd(lambda e: e.dma_start(out=self.identb[:], in_=self.ident_d), inc="sua")
        self.gd(lambda e: e.dma_start(out=self.mask4[:], in_=self.mask4_d), inc="sua")
        self.gd(lambda e: e.dma_start(out=self.wq[:], in_=self.wq_d), inc="sua")
        self.gd(lambda e: e.dma_start(out=self.wk[:], in_=self.wk_d), inc="sua")
        self.gd(lambda e: e.dma_start(out=self.wv[:], in_=self.wv_d), inc="sua")
        evA = self.gd(lambda e: e.dma_start(out=self.wg[:], in_=self.wg_d), inc="sua")
        self.ev_mask = self.ev_attnw = evA
        self.gd(lambda e: e.dma_start(out=self.wo_s, in_=self.wo_d), inc="sub")
        self.gd(lambda e: e.dma_start(out=self.w1_s, in_=self.w1_d), inc="sub")
        evB = self.gd(lambda e: e.dma_start(out=self.w2_s, in_=self.w2_d), inc="sub")
        self.ev_wo_s = self.ev_w1_s = self.ev_w2_s = evB
        evC = self.gd(lambda e: e.dma_start(out=self.w3_s, in_=self.w3_d), inc="suw")
        self.ev_w3_s = evC
        self.ev_h1_s = [None] * 4
        self.ev_h2_s = [None] * 4
        self.v(lambda e: e.memset(self.junk[:], 0.0))
        self.v(lambda e: e.memset(self.S32[:], 0.0))
        self.ev_spad = self.v(lambda e: e.memset(self.spad[:], 0.0), inc="dv")
        self.xv = self.x_in.rearrange("(s tg p) i -> s p tg i", p=128, tg=4)
        self.outv = self.out.rearrange("(s tg p) i -> s p tg i", p=128, tg=4)
        self.ev_xload = [None] * (NSUB + 2)
        for s in range(2):
            self.ev_xload[s] = self.qa_(
                lambda e, s=s: e.dma_start(out=self.xsb[s % 2][:], in_=self.xv[s]),
                inc="xl%d" % (s % 2))
        self.ev_rope_read = None
        self.ev_rope = self._load_rope(0, deps=[])
        self.bank_free = {k: None for k in ("P0", "P1", "P23", "P45", "PT", "PS")}
        self.ev_arena_free = None
        self.ev_hb_free = None
        self.ev_small_free = [None] * 4
        self.ev_hslab_free = [None, None]
        self.ev_abf_free = [None, None]
        self.ev_x2fm_free = None
        self.ev_lst2_free = None
        self.ev_lst_free = None
        self.pending_ln1 = {}
        self.ev_smul = None
        self._gate_free = None

    def _load_rope(self, tloc, deps):
        evs = None
        for i, tab in enumerate((self.ropeQc_d, self.ropeQs_d, self.ropeKc_d, self.ropeKs_d)):
            evs = self.qa_(
                lambda e, i=i, tab=tab, tloc=tloc: e.dma_start(
                    out=self.rope[:, i, :], in_=tab[tloc:tloc + 128, :]),
                waits=deps if i == 0 else (), inc="rp")
        return evs


    def _attn_front(self, s, tg):
        xs = self.xsb[s % 2]
        x_tg = xs[:, tg, :]
        st = {"x_tg": x_tg}
        m = self.lnm[:, tg:tg + 1]
        r = self.lnm[:, 4 + tg:5 + tg]
        prev = self._atn
        ev_apply = self.v(
            lambda e, x_tg=x_tg, m=m, r=r: e.tensor_scalar(
                self.xln_tm, x_tg, m, r, OP.subtract, OP.mult),
            waits=emax(self.ev_ln1_chain,
                       self.ev_arena_free if tg == 0 else None), inc="dv")
        self.tap("t_xln", self.xln_tm, emax(ev_apply))
        ev_tp = None
        for c in range(8):
            ev_tp = self.t(
                lambda e, c=c: e.transpose(self.PT[:, c * 128:(c + 1) * 128],
                                           self.xln_tm[:, c * 128:(c + 1) * 128],
                                           self.identb[:]),
                waits=emax(ev_apply, self.bank_free["PT"], self.ev_attnw) if c == 0 else (),
                inc="pe" if c == 7 else None)
        st["ev_tp"] = ev_tp
        ev_xfm = None
        for c in range(8):
            ev_xfm = self.v(
                lambda e, c=c: e.tensor_copy(self.xlnFM[:, c * 128:(c + 1) * 128],
                                             self.PT[:, c * 128:(c + 1) * 128]),
                waits=emax(ev_tp, prev["ev_vg"] if prev else None) if c == 0 else (),
                inc="dv" if c == 7 else None)
        self.bank_free["PT"] = ev_xfm
        st["ev_xfm"] = ev_xfm
        st["evQ"] = self._proj(self.P0[:], self.wq, 0, 512, ev_xfm, self.bank_free["P0"])
        st["evK"] = self._proj(self.P1[:], self.wk, 0, 512, ev_xfm, self.bank_free["P1"])
        return st

    def _proj(self, psum, wsb, n0, n1, ev_xfm, extra):
        ev = None
        for ic in range(8):
            ev = self.t(
                lambda e, ic=ic, psum=psum, wsb=wsb, n0=n0, n1=n1: e.matmul(
                    psum, self.xlnFM[:, ic * 128:(ic + 1) * 128],
                    wsb[:, ic, n0:n1], start=(ic == 0), stop=(ic == 7)),
                waits=emax(ev_xfm, extra) if ic == 0 else (),
                inc="pe" if ic == 7 else None)
        return ev

    def _attn_mid_a(self, s, tg, st):
        st["evVa"] = self._proj(self.P23[:, 0:512], self.wv, 0, 512, st["ev_xfm"], self.bank_free["P23"])
        st["evVb"] = self._proj(self.P23[:, 512:1024], self.wv, 512, 1024, st["ev_xfm"], None)
        evGa = self._proj(self.P45[:, 0:512], self.wg, 0, 512, st["ev_xfm"], self.bank_free["P45"])
        evGb = self._proj(self.P45[:, 512:1024], self.wg, 512, 1024, st["ev_xfm"], None)
        st["ev_vg"] = evGb
        self.a(lambda e: e.activation(self.Gs[:, 0:512], self.P45[:, 0:512], AF.Silu),
               waits=emax(evGa))
        st["evsl"] = self.a(
            lambda e: e.activation(self.Gs[:, 512:1024], self.P45[:, 512:1024], AF.Silu),
            waits=emax(evGb), inc="ac")

    def _attn_mid_b(self, s, tg, st):
        self.g(lambda e: e.tensor_mul(self.Gsw, self.Gs, self.gnwB[:]),
               waits=emax(st["evsl"], self.ev_consts, self._gate_free))
        evgb = self.g(lambda e: e.tensor_mul(self.Gsb, self.Gs, self.gnbB[:]), inc="po")
        st["evgb"] = evgb

        def rot(psum, ci, si, outtm, dep, war=None):
            cosT = self.rope[:, ci, :]
            sinT = self.rope[:, si, :]
            pv = psum.rearrange("p (h half d) -> p h half d", h=8, half=2, d=32)
            sv = sinT.rearrange("p (h half d) -> p h half d", h=8, half=2, d=32)
            qbv = self.qb32.rearrange("p (h half d) -> p h half d", h=8, half=2, d=32)
            self.v(lambda e, psum=psum, cosT=cosT: e.tensor_mul(self.qa32, psum, cosT),
                   waits=emax(self.ev_rope, dep, war))
            self.v(lambda e, pv=pv, sv=sv, qbv=qbv: e.tensor_tensor(
                qbv[:, :, 0, :], pv[:, :, 1, :], sv[:, :, 0, :], OP.mult))
            ev_parts = self.v(lambda e, pv=pv, sv=sv, qbv=qbv: e.tensor_tensor(
                qbv[:, :, 1, :], pv[:, :, 0, :], sv[:, :, 1, :], OP.mult), inc="dv")
            return self.g(lambda e, outtm=outtm: e.tensor_add(outtm, self.qa32, self.qb32),
                          waits=emax(ev_parts), inc="po")

        evq = rot(self.P0[:], 0, 1, self.qtm, st["evQ"])
        self.bank_free["P0"] = evq
        evk = rot(self.P1[:], 2, 3, self.ktm, st["evK"], war=evq)
        self.bank_free["P1"] = evk
        st["evq"], st["evk"] = evq, evk
        self.ev_rope_read = evk
        self.tap("t_qtm", self.qtm, emax(evq))
        self.tap("t_ktm", self.ktm, emax(evk))
        nxt = ((s % SUB_PER_SEQ) * STOK + (tg + 1) * 128) % T
        self.ev_rope = self._load_rope(nxt, deps=emax(evk))

        self.v(lambda e: e.tensor_copy(self.v_tm[:, 0:512], self.P23[:, 0:512]),
               waits=emax(st["evVa"]))
        evv = self.v(lambda e: e.tensor_copy(self.v_tm[:, 512:1024], self.P23[:, 512:1024]),
                     waits=emax(st["evVb"]), inc="dv")
        self.bank_free["P23"] = evv
        st["evv"] = evv
        self.tap("t_vtm", self.v_tm, emax(evv))

        ev_tp2 = None
        for i in range(8):
            src = self.qtm if i < 4 else self.ktm
            cc = i % 4
            ev_tp2 = self.t(
                lambda e, src=src, cc=cc, i=i: e.transpose(
                    self.PT[:, i * 128:(i + 1) * 128],
                    src[:, cc * 128:(cc + 1) * 128], self.identb[:]),
                waits=emax(evq, evk, self.bank_free["PT"]) if i == 0 else (),
                inc="pe" if i == 7 else None)
        ev_fm = None
        for i in range(8):
            dst = self.QFM if i < 4 else self.KFM
            cc = i % 4
            ev_fm = self.v(
                lambda e, dst=dst, cc=cc, i=i: e.tensor_copy(
                    dst[:, cc * 128:(cc + 1) * 128], self.PT[:, i * 128:(i + 1) * 128]),
                waits=emax(ev_tp2) if i == 0 else (), inc="dv" if i == 7 else None)
        self.bank_free["PT"] = ev_fm
        st["ev_fm"] = ev_fm

    def _attn_back1a(self, s, tg, st):
        ev_fm = st["ev_fm"]
        evv = st["evv"]
        ev_sc = [None, None]
        for h in range(8):
            p = h // 2
            base = (h % 2) * 64
            psum = self.P0 if h % 2 == 0 else self.P1
            col = (h // 2) * 128
            ev = self.t(
                lambda e, p=p, base=base, psum=psum, col=col: e.matmul(
                    psum[:, col:col + 128],
                    self.KFM[base:base + 64, p * 128:(p + 1) * 128],
                    self.QFM[base:base + 64, p * 128:(p + 1) * 128],
                    start=True, stop=True),
                waits=emax(ev_fm,
                           self.bank_free["P0"] if h == 0 else None,
                           self.bank_free["P1"] if h == 1 else None),
                inc="pe" if h >= 6 else None)
            if h >= 6:
                ev_sc[h % 2] = ev

        self.v(lambda e: e.tensor_mul(self.scM[:, 0:512], self.P0[:], self.mask4[:]),
               waits=emax(ev_sc[0], self.ev_mask))
        evm = self.v(lambda e: e.tensor_mul(self.scM[:, 512:1024], self.P1[:], self.mask4[:]),
                     waits=emax(ev_sc[1]), inc="dv")
        self.bank_free["P0"] = evm
        self.bank_free["P1"] = evm
        self.tap("t_scM", self.scM, emax(evm))

        ev_o = None
        for h in range(8):
            p = h // 2
            col = h * 128
            smoff = (h % 2) * 512 + (h // 2) * 128
            self.t(lambda e, h=h, col=col, smoff=smoff: e.matmul(
                self.P23[:, col:col + 128],
                self.scM[:, smoff:smoff + 128],
                self.v_tm[:, h * 128:(h + 1) * 128],
                start=True, stop=False),
                waits=emax(evm, evv, self.bank_free["P23"]) if h == 0 else ())
            ev_o = self.t(lambda e, p=p, col=col, h=h: e.matmul(
                self.P23[:, col:col + 128],
                self.QFM[:, p * 128:(p + 1) * 128],
                self.spad[:, h * 128:(h + 1) * 128],
                start=False, stop=True),
                waits=emax(self.ev_spad) if h == 0 else (),
                inc="pe" if h == 7 else None)
        st["ev_o"] = ev_o

        self.v(lambda e: e.tensor_copy(self.o_sb[:, 0:512], self.P23[:, 0:512]),
               waits=emax(ev_o))
        ev_oev = self.v(lambda e: e.tensor_copy(self.o_sb[:, 512:1024], self.P23[:, 512:1024]),
                        inc="dv")
        self.bank_free["P23"] = ev_oev
        st["ev_oev"] = ev_oev

        ev_u = None
        for h in range(8):
            ev_u = self.t(lambda e, h=h: e.matmul(
                self.PS[:, h * 128:(h + 1) * 128],
                self.ktm[:, h * 64:(h + 1) * 64],
                self.v_tm[:, h * 128:(h + 1) * 128],
                start=True, stop=True),
                waits=emax(st["evsl"], st["evk"], evv) if h == 0 else (),
                inc="pe" if h == 7 else None)
        self._prev_u = ev_u

        ev_sadd = self.v(lambda e: e.tensor_add(self.S32[:], self.S32[:], self.PS[:]),
                         waits=emax(ev_u, self.ev_smul), inc="dv")
        evs2 = self.g(lambda e: e.tensor_mul(self.S32[:], self.S32[:], self.gCB[0:64, :]),
                      waits=emax(ev_sadd), inc="po")
        self.ev_smul = evs2
        self.bank_free["P45"] = evs2
        spv = self.spad.rearrange("p (h v) -> p h v", h=8, v=128)
        sv32 = self.S32.rearrange("p (h v) -> p h v", h=8, v=128)
        self.g(lambda e, spv=spv, sv32=sv32: e.tensor_copy(spv[0:64, 0::2, :], sv32[:, 0::2, :]),
               waits=emax(ev_o))
        self.ev_spad = self.g(
            lambda e, spv=spv, sv32=sv32: e.tensor_copy(spv[64:128, 1::2, :], sv32[:, 1::2, :]),
            inc="po")

    def _attn_gn_stats(self, s, tg, st):
        ev_st = None
        for h in range(8):
            col = h * 128
            self.stat_op(self.o_sb[:, col:col + 128], AF.Identity,
                         self.gst[:, h:h + 1],
                         extra_waits=emax(st["ev_oev"]) if h == 0 else ())
            ev_st = self.stat_op(self.o_sb[:, col:col + 128], AF.Square,
                                 self.gst[:, 8 + h:9 + h])
        st["ev_st"] = ev_st

    def _attn_gn_chain(self, s, tg, st):
        ev_st = st["ev_st"]
        gm = self.gmr[:, 0:8]
        gv = self.gmr[:, 8:16]
        gt = self.rs_t[:, 24:32]
        self.v(lambda e: e.tensor_scalar(gm, self.gst[:, 0:8], 1.0 / 128.0, None, OP.mult),
               waits=emax(ev_st))
        self.v(lambda e: e.tensor_scalar(gv, self.gst[:, 8:16], 1.0 / 128.0, None, OP.mult))
        self.v(lambda e: e.tensor_mul(gt, gm, gm))
        self.spacer()
        self.v(lambda e: e.scalar_tensor_tensor(gv, gt, -1.0, gv, OP.mult, OP.add))
        st["ev_rs"] = self.rsqrt(gv, gv, 8, so=0)

    def _attn_back2(self, s, tg, st, st_next):
        xs = self.xsb[s % 2]
        x_tg = st["x_tg"]
        gm = self.gmr[:, 0:8]
        gv = self.gmr[:, 8:16]
        ev_ap = None
        for h in range(8):
            col = h * 128
            ev_ap = self.v(lambda e, h=h, col=col: e.tensor_scalar(
                self.on_all[:, col:col + 128], self.o_sb[:, col:col + 128],
                gm[:, h:h + 1], gv[:, h:h + 1], OP.subtract, OP.mult),
                waits=emax(st["ev_rs"]) if h == 0 else (),
                inc="dv" if h == 7 else None)
        self.v(lambda e: e.tensor_mul(self.og, self.on_all, self.Gsw),
               waits=emax(st["evgb"]))
        ev_og = self.v(lambda e: e.tensor_add(self.og, self.og, self.Gsb), inc="dv")
        self._gate_free = ev_og
        self.tap("t_og", self.og, emax(ev_og))

        ev_tp3 = None
        for c in range(8):
            ev_tp3 = self.t(
                lambda e, c=c: e.transpose(self.PT[:, c * 128:(c + 1) * 128],
                                           self.og[:, c * 128:(c + 1) * 128],
                                           self.identb[:]),
                waits=emax(ev_og, self.bank_free["PT"]) if c == 0 else (),
                inc="pe" if c == 7 else None)
        ev_ogfm = None
        for c in range(8):
            ev_ogfm = self.v(
                lambda e, c=c: e.tensor_copy(self.ogFM[:, c * 128:(c + 1) * 128],
                                             self.PT[:, c * 128:(c + 1) * 128]),
                waits=emax(ev_tp3) if c == 0 else (), inc="dv" if c == 7 else None)
        self.bank_free["PT"] = ev_ogfm
        self.tap("t_ogfm", self.ogFM, emax(ev_ogfm))

        ev_wo = None
        for cc in range(8):
            slot = cc % 4
            ev_dma = self.qa_(
                lambda e, cc=cc, slot=slot: e.dma_start(
                    out=self.small[slot], in_=self.wo_s[cc]),
                waits=emax(self.ev_wo_s, self.ev_small_free[slot]), inc="sm%d" % slot)
            if cc == 0:
                self.tap("t_wo0", self.small[0], emax(ev_dma))
            for half in range(2):
                ev_wo = self.t(lambda e, cc=cc, slot=slot, half=half: e.matmul(
                    self.P45[:, half * 512:(half + 1) * 512],
                    self.ogFM[:, cc * 128:(cc + 1) * 128],
                    self.small[slot][:, half * 512:(half + 1) * 512],
                    start=(cc == 0), stop=(cc == 7)),
                    waits=emax(ev_ogfm, ev_dma,
                               (st_next["evsl"] if st_next is not None
                                else self.bank_free["P45"]) if cc == 0 else None)
                    if half == 0 else (),
                    inc="pe" if half == 1 else None)
            self.ev_small_free[slot] = ev_wo

        ev_res = self.v(lambda e, x_tg=x_tg: e.tensor_add(x_tg, x_tg, self.P45[:]),
                        waits=emax(ev_wo), inc="dv")
        self.bank_free["P45"] = ev_res
        if tg == 3:
            self.tap("t_x1", xs[:, 0, :], emax(ev_res))

        self.stat_op(x_tg, AF.Identity, self.lst2[:, tg * 2:tg * 2 + 1],
                     extra_waits=emax(ev_res, self.ev_lst2_free))
        ev_l2 = self.stat_op(x_tg, AF.Square, self.lst2[:, tg * 2 + 1:tg * 2 + 2])
        self.ev_ln2_stats = ev_l2
        self.ev_attn_done = ev_res

    def _attn_phase(self, s):
        self._atn = None
        st = self._attn_front(s, 0)
        self._atn = st
        self._attn_mid_a(s, 0, st)
        self._attn_mid_b(s, 0, st)
        for tg in range(4):
            st = self._atn
            self._attn_back1a(s, tg, st)
            self._attn_gn_stats(s, tg, st)
            nst = None
            if tg < 3:
                nst = self._attn_front(s, tg + 1)
            self._attn_gn_chain(s, tg, st)
            if tg < 3:
                self._attn_mid_a(s, tg + 1, nst)
            self._attn_back2(s, tg, st, nst)
            if tg < 3:
                self._atn = nst
                self._attn_mid_b(s, tg + 1, nst)


    def _ln_chain(self, lst, mdst, rdst, ev_stats, so):
        lv = lst[:, 0:8].rearrange("p (tg c) -> p tg c", tg=4, c=2)
        t4 = self.rs_t[:, so + 8:so + 12]
        b4 = self.rs_b[:, so + 8:so + 12]
        self.v(lambda e: e.tensor_scalar(mdst, lv[:, :, 0], 1.0 / 1024.0, None, OP.mult),
               waits=emax(ev_stats, self.ev_consts))
        self.v(lambda e: e.tensor_scalar(b4, lv[:, :, 1], 1.0 / 1024.0, None, OP.mult))
        self.v(lambda e: e.tensor_mul(t4, mdst, mdst))
        self.spacer()
        self.v(lambda e: e.scalar_tensor_tensor(b4, t4, -1.0, b4, OP.mult, OP.add))
        return self.rsqrt(rdst, b4, 4, so=so)

    def _ln1_block(self, s):
        xs = self.xsb[s % 2]
        ev_stats = None
        for tg in range(4):
            x_tg = xs[:, tg, :]
            self.stat_op(x_tg, AF.Identity, self.lst[:, tg * 2:tg * 2 + 1],
                         extra_waits=emax(self.ev_xload[s], self.ev_lst_free) if tg == 0 else ())
            ev_stats = self.stat_op(x_tg, AF.Square, self.lst[:, tg * 2 + 1:tg * 2 + 2])
        ev = self._ln_chain(self.lst, self.lnm[:, 0:4], self.lnm[:, 4:8],
                            ev_stats, so=0)
        self.ev_lst_free = ev
        return ev

    def _subtile(self, s):
        xs = self.xsb[s % 2]
        if s % SUB_PER_SEQ == 0 and s > 0:
            self.v(lambda e: e.memset(self.S32[:], 0.0), waits=emax(self.ev_smul))
            self.ev_spad = self.v(lambda e: e.memset(self.spad[:], 0.0), inc="dv")
            self.ev_smul = self.ev_spad

        if self.pending_ln1.get(s) is None:
            self.pending_ln1[s] = self._ln1_block(s)
        self.ev_ln1_chain = self.pending_ln1[s]

        if BUILD_PHASE == "ln1":
            ev_store = self.gd(lambda e, s=s: e.dma_start(out=self.outv[s], in_=xs[:]),
                               waits=emax(self.ev_ln1_chain), inc="st%d" % (s % 2))
            if s + 2 < BUILD_NSUB:
                self.ev_xload[s + 2] = self.qa_(
                    lambda e, s=s: e.dma_start(out=self.xsb[s % 2][:], in_=self.xv[s + 2]),
                    waits=emax(ev_store), inc="xl%d" % (s % 2))
            return

        self._attn_phase(s)

        if BUILD_PHASE == "attn":
            ev_store = self.gd(lambda e, s=s: e.dma_start(out=self.outv[s], in_=xs[:]),
                               waits=emax(self.ev_attn_done), inc="st%d" % (s % 2))
            if s + 2 < BUILD_NSUB:
                self.ev_xload[s + 2] = self.qa_(
                    lambda e, s=s: e.dma_start(out=self.xsb[s % 2][:], in_=self.xv[s + 2]),
                    waits=emax(ev_store), inc="xl%d" % (s % 2))
            self.ev_arena_free = self.ev_attn_done
            return

        ev_ln2_chain = self._ln_chain(self.lst2, self.lnm[:, 8:12], self.lnm[:, 12:16],
                                      self.ev_ln2_stats, so=4)
        self.ev_lst2_free = ev_ln2_chain
        ev_x2 = None
        for tg in range(4):
            x_tg = xs[:, tg, :]
            m = self.lnm[:, 8 + tg:9 + tg]
            r = self.lnm[:, 12 + tg:13 + tg]
            ev_ap = self.v(lambda e, x_tg=x_tg, m=m, r=r: e.tensor_scalar(
                self.xln_tm, x_tg, m, r, OP.subtract, OP.mult),
                waits=emax(ev_ln2_chain), inc="dv")
            ev_tp = None
            for c in range(8):
                ev_tp = self.t(lambda e, c=c: e.transpose(
                    self.PT[:, c * 128:(c + 1) * 128],
                    self.xln_tm[:, c * 128:(c + 1) * 128], self.identb[:]),
                    waits=emax(ev_ap, self.bank_free["PT"]) if c == 0 else (),
                    inc="pe" if c == 7 else None)
            for c in range(8):
                ev_x2 = self.v(lambda e, c=c, tg=tg: e.tensor_copy(
                    self.x2FM[:, c, tg * 128:(tg + 1) * 128],
                    self.PT[:, c * 128:(c + 1) * 128]),
                    waits=emax(ev_tp, self.ev_x2fm_free if (tg == 0 and c == 0) else None)
                    if c == 0 else (),
                    inc="dv" if c == 7 else None)
            self.bank_free["PT"] = ev_x2
        self.tap("t_x2fm", self.x2FM[:, 0:4, :], emax(ev_x2))
        ev_arena = emax(self.ev_attn_done, ev_x2)

        pb_even = (self.P0[:], "P0", self.P1[:], "P1")
        pb_odd = (self.P23[:, 0:512], "P23", self.P23[:, 512:1024], "P23")
        ev_y = None
        ev_y1mm_last = None
        for oc in range(32):
            sl1 = oc % 2
            sl2 = 2 + oc % 2
            evd1 = self.q(lambda e, oc=oc, sl=sl1: e.dma_start(
                out=self.small_iv[sl], in_=self.w1_s[:, oc]),
                waits=emax(self.ev_w1_s, self.ev_small_free[sl1]), inc="sm%d" % sl1)
            evd2 = self.q(lambda e, oc=oc, sl=sl2: e.dma_start(
                out=self.small_iv[sl], in_=self.w2_s[:, oc]),
                waits=emax(self.ev_w2_s, self.ev_small_free[sl2]), inc="sm%d" % sl2)
            pa, ka, pb, kb = pb_even if oc % 2 == 0 else pb_odd
            ev_m1 = None
            for ic in range(8):
                ev_m1 = self.t(lambda e, ic=ic, pa=pa, sl=sl1: e.matmul(
                    pa, self.small_iv[sl][:, ic, :], self.x2FM[:, ic, :],
                    start=(ic == 0), stop=(ic == 7)),
                    waits=emax(evd1, ev_x2, self.bank_free[ka]) if ic == 0 else (),
                    inc="pe" if ic == 7 else None)
            ev_m2 = None
            for ic in range(8):
                ev_m2 = self.t(lambda e, ic=ic, pb=pb, sl=sl2: e.matmul(
                    pb, self.small_iv[sl][:, ic, :], self.x2FM[:, ic, :],
                    start=(ic == 0), stop=(ic == 7)),
                    waits=emax(evd2, self.bank_free[kb]) if ic == 0 else (),
                    inc="pe" if ic == 7 else None)
            ev_y1mm_last = ev_m2
            self.ev_small_free[sl1] = ev_m1
            self.ev_small_free[sl2] = ev_m2
            asl = oc % 2
            ev_a = self.a(lambda e, oc=oc, pa=pa, asl=asl: e.activation(
                self.abf[:, asl, :], pa, AF.Silu, scale=self.s1[:, oc:oc + 1]),
                waits=emax(ev_m1, self.ev_abf_free[asl]), inc="ac")
            ev_y = self.v(lambda e, oc=oc, pb=pb, asl=asl: e.scalar_tensor_tensor(
                self.y1[:, oc, :], pb, self.s2[:, oc:oc + 1], self.abf[:, asl, :],
                OP.mult, OP.mult),
                waits=emax(ev_m2, ev_a, *(ev_arena if oc == 0 else [])), inc="dv")
            self.ev_abf_free[asl] = ev_y
            self.bank_free[ka] = ev_a
            self.bank_free[kb] = ev_y
        ev_y1_all = ev_y
        self.tap("t_y1", self.y1[:, 0, :], emax(ev_y1_all))

        def h_stage(scr, scr_evs, src_view, dst_store, s_ap, src_ready, dst_free,
                    fp32_mode=False, fp32_src=None, cast_to=None, cast_sem=None):
            ev_g = None
            p45_free = [self.bank_free["P45"], self.bank_free["P45"]]
            for oc in range(32):
                slot = oc % 2
                if fp32_mode:
                    evd = self.gd(lambda e, oc=oc, slot=slot, scr=fp32_src: e.dma_start(
                        out=self.hslab[slot], in_=scr[:, oc]),
                        waits=emax(self.ev_hslab_free[slot]), inc="hf%d" % slot)
                    if cast_to is not None:
                        self.gd(lambda e, oc=oc: e.dma_start(
                            out=cast_to[0][:, oc], in_=cast_to[1][:, oc]), inc=cast_sem)
                else:
                    evd = self.q(lambda e, oc=oc, slot=slot, scr=scr: e.dma_start(
                        out=self.hslab[slot], in_=scr[:, oc]),
                        waits=emax(self.ev_hslab_free[slot], scr_evs[oc // 8]), inc="hs%d" % slot)
                psum = self.P45[:, 0:512] if oc % 2 == 0 else self.P45[:, 512:1024]
                ev_m = None
                for ic in range(32):
                    ev_m = self.t(lambda e, ic=ic, slot=slot, psum=psum,
                                  src_view=src_view: e.matmul(
                        psum, self.hslab[slot][:, ic, :], src_view[:, ic, :],
                        start=(ic == 0), stop=(ic == 31)),
                        waits=emax(evd, src_ready, p45_free[oc % 2]) if ic == 0 else (),
                        inc="pe" if ic == 31 else None)
                self.ev_hslab_free[slot] = ev_m
                ev_g = self.a(lambda e, oc=oc, psum=psum, s_ap=s_ap,
                              dst_store=dst_store: e.activation(
                    dst_store[:, oc, :], psum, AF.Gelu, scale=s_ap[:, oc:oc + 1]),
                    waits=emax(ev_m, dst_free if oc == 0 else None), inc="ac")
                p45_free[oc % 2] = ev_g
            self.bank_free["P45"] = ev_g
            return ev_g

        if s <= 1:
            ev_h1 = h_stage(self.h1_s, self.ev_h1_s, self.y1, self.hb, self.s1h,
                            ev_y1_all, self.ev_hb_free, fp32_mode=True,
                            fp32_src=self.h1_d,
                            cast_to=(self.h1_s, self.h1_d) if s == 0 else None,
                            cast_sem="suc")
        else:
            ev_h1 = h_stage(self.h1_s, self.ev_h1_s, self.y1, self.hb, self.s1h,
                            ev_y1_all, self.ev_hb_free)
        self.tap("t_hb", self.hb[:, 0, :], emax(ev_h1))
        if s <= 1:
            ev_h2 = h_stage(self.h2_s, self.ev_h2_s, self.hb, self.y1, self.s2h,
                            ev_h1, None, fp32_mode=True, fp32_src=self.h2_d,
                            cast_to=(self.h2_s, self.h2_d) if s == 0 else None,
                            cast_sem="sud")
        else:
            ev_h2 = h_stage(self.h2_s, self.ev_h2_s, self.hb, self.y1, self.s2h,
                            ev_h1, None)
        self.ev_hb_free = ev_h2
        self.ev_x2fm_free = ev_h2

        if s + 1 < BUILD_NSUB:
            self.pending_ln1[s + 1] = self._ln1_block(s + 1)

        w3psum = [(self.P0[:], "P0"), (self.P1[:], "P1"),
                  (self.P23[:, 0:512], "P23"), (self.P23[:, 512:1024], "P23")]
        ev_evac = None
        for half in range(2):
            ev_mm = [None] * 4
            for oc in range(32):
                sl = oc % 4
                evd = self.q(lambda e, oc=oc, sl=sl, half=half: e.dma_start(
                    out=self.small[sl][:, 0:512],
                    in_=self.w3_s[:, oc, half * 512:(half + 1) * 512]),
                    waits=emax(self.ev_w3_s, self.ev_small_free[sl]), inc="sm%d" % sl)
                for tg in range(4):
                    psum, key = w3psum[tg]
                    ev_mm[tg] = self.t(lambda e, oc=oc, tg=tg, sl=sl, psum=psum: e.matmul(
                        psum, self.y1[:, oc, tg * 128:(tg + 1) * 128],
                        self.small[sl][:, 0:512],
                        start=(oc == 0), stop=(oc == 31)),
                        waits=emax(evd, ev_h2,
                                   self.bank_free[key] if oc == 0 else None)
                        if oc == 0 else emax(evd),
                        inc="pe" if tg == 3 else None)
                self.ev_small_free[sl] = ev_mm[3]
            for tg in range(4):
                psum, key = w3psum[tg]
                x_sl = xs[:, tg, half * 512:(half + 1) * 512]
                self.v(lambda e, psum=psum, half=half: e.tensor_mul(
                    self.w3scr[:, 0:512], psum, self.s3B[:, half * 512:(half + 1) * 512]),
                    waits=emax(ev_mm[3]))
                ev_evac = self.v(lambda e, x_sl=x_sl: e.tensor_add(
                    x_sl, x_sl, self.w3scr[:, 0:512]), inc="dv")
                self.bank_free[key] = ev_evac
            for sl in range(4):
                self.ev_small_free[sl] = ev_mm[3]

        if s == 0:
            self.ev_h1_s = [("suc", self.s.count["suc"])] * 4
            self.ev_h2_s = [("sud", self.s.count["sud"])] * 4
        ev_store = self.gd(lambda e, s=s: e.dma_start(out=self.outv[s], in_=xs[:]),
                          waits=emax(ev_evac), inc="st%d" % (s % 2))
        if s + 2 < NSUB:
            self.ev_xload[s + 2] = self.qa_(
                lambda e, s=s: e.dma_start(out=self.xsb[s % 2][:], in_=self.xv[s + 2]),
                waits=emax(ev_store), inc="xl%d" % (s % 2))
        self.ev_arena_free = ev_mm[3]

    def _replay(self):
        nc = self.nc
        import contextlib
        with contextlib.ExitStack() as es:
            semh = {n: es.enter_context(nc.semaphore(n)) for n in self.s.count}
            block = es.enter_context(nc.Block())

            def mk(eng_ops):
                def run(eng):
                    for ws, fn, inc, inc_n in eng_ops:
                        for (sname, val) in ws:
                            eng.wait_ge(semh[sname], val)
                        inst = fn(eng)
                        if inc is not None:
                            inst.then_inc(semh[inc], inc_n)
                return run

            block.tensor(mk(self.s.ops["tensor"]))
            block.vector(mk(self.s.ops["vector"]))
            block.scalar(mk(self.s.ops["scalar"]))
            block.sync(mk(self.s.ops["sync"]))
            block.gpsimd(mk(self.s.ops["gpsimd"]))


_CACHE = {}


def build_program():
    if "nc" not in _CACHE:
        b = Builder()
        _CACHE["nc"] = b.build()
    return _CACHE["nc"]


def kernel(**inputs):
    nc = build_program()
    prep = host_prepare(inputs)
    x = np.ascontiguousarray(inputs["L0_attn_x"].reshape(B, T, HID), np.float32)
    in_maps = []
    for c in range(NCORES):
        m = {k: v for k, v in prep.items()}
        m["x"] = np.ascontiguousarray(x[2 * c:2 * c + 2].reshape(TOK, HID))
        in_maps.append(m)
    res = run_bass_kernel_spmd(nc, in_maps, core_ids=list(range(NCORES)))
    out = np.empty((B, T, HID), np.float32)
    for c in range(NCORES):
        out[2 * c:2 * c + 2] = res.results[c]["out"].reshape(2, T, HID)
    return out
